# version 01
# baseline (speedup 1.0000x reference)
import numpy as np
import concourse.bass as bass
import concourse.mybir as mybir

F32 = mybir.dt.float32
AF = mybir.ActivationFunctionType
ALU = mybir.AluOpType
AX = mybir.AxisListType

S = 2048
D = 1024
H = 16
F = 4096
NT = S // 128
NC4 = 4
QC = 8
VW = 66
SHIFT = 20.0
C_NSHIFT, C_QSCALE, C_ONE, C_ZERO = 0, 1, 2, 3


def host_inputs(x, gamma, wqkv, wout, router_w, w1, b1, w2, b2):
    d = {}
    d['x'] = np.ascontiguousarray(np.asarray(x, np.float32))
    d['gamma_c'] = np.ascontiguousarray(np.asarray(gamma, np.float32).reshape(8, 128).T)
    d['wqkv_bl'] = np.ascontiguousarray(
        np.asarray(wqkv, np.float32).reshape(8, 128, 24, 128).transpose(2, 1, 0, 3))
    d['wout'] = np.ascontiguousarray(np.asarray(wout, np.float32))
    d['rw'] = np.ascontiguousarray(np.asarray(router_w, np.float32))
    d['w1_bl'] = np.ascontiguousarray(
        np.asarray(w1, np.float32).reshape(8, 128, 32, 128).transpose(2, 1, 0, 3))
    d['b1_c'] = np.ascontiguousarray(np.asarray(b1, np.float32).reshape(32, 128).T)
    d['w2'] = np.ascontiguousarray(np.asarray(w2, np.float32))
    d['b2_bc'] = np.ascontiguousarray(np.broadcast_to(np.asarray(b2, np.float32), (128, D)))
    d['ident'] = np.eye(128, dtype=np.float32)
    cst = np.zeros((128, 4), np.float32)
    cst[:, C_NSHIFT] = -SHIFT
    cst[:, C_QSCALE] = 0.125
    cst[:, C_ONE] = 1.0
    d['cst'] = cst
    return d


class Ctr:
    def __init__(self, sem):
        self.sem = sem
        self.n = 0

    def inc(self, inst, by=1):
        inst.then_inc(self.sem, by)
        self.n += by
        return self.n

    def wait(self, eng, target=None):
        eng.wait_ge(self.sem, self.n if target is None else target)


class ChEng:

    def __init__(self, eng, ctr):
        self.e = eng
        self.c = ctr

    def op(self, name, *a, **k):
        self.e.wait_ge(self.c.sem, self.c.n)
        inst = getattr(self.e, name)(*a, **k)
        self.c.inc(inst)
        return self.c.n

    def wait_ge(self, sem, n):
        self.e.wait_ge(sem, n)

    def wait_tok(self, tok):
        c, n = tok
        self.e.wait_ge(c.sem, n)

    @property
    def tok(self):
        return (self.c, self.c.n)


class DmaStream:

    def __init__(self, nc, es, name, nbuf=1):
        self.sems = [Ctr(es.enter_context(nc.semaphore(f'{name}{i}'))) for i in range(nbuf)]
        self.nbuf = nbuf
        self.i = 0

    def issue(self, eng, pairs):
        c = self.sems[self.i % self.nbuf]
        eng.wait_ge(c.sem, c.n)
        for out, in_ in pairs:
            c.inc(eng.dma_start(out, in_), 16)
        self.i += 1
        return (c, c.n)


def wait_tok(eng, tok):
    c, n = tok
    eng.wait_ge(c.sem, n)


def build():
    phases = (1, 2, 3)
    debug = ()
    nc = bass.Bass('TRN2', target_bir_lowering=False)

    x_h = nc.dram_tensor('x', [S, D], F32, kind='ExternalInput').ap()
    gamma_h = nc.dram_tensor('gamma_c', [128, 8], F32, kind='ExternalInput').ap()
    wqkv_h = nc.dram_tensor('wqkv_bl', [24, 128, 8, 128], F32, kind='ExternalInput').ap()
    wout_h = nc.dram_tensor('wout', [D, D], F32, kind='ExternalInput').ap()
    rw_h = nc.dram_tensor('rw', [D, 8], F32, kind='ExternalInput').ap()
    w1_h = nc.dram_tensor('w1_bl', [32, 128, 8, 128], F32, kind='ExternalInput').ap()
    b1_h = nc.dram_tensor('b1_c', [128, 32], F32, kind='ExternalInput').ap()
    w2_h = nc.dram_tensor('w2', [F, D], F32, kind='ExternalInput').ap()
    b2_h = nc.dram_tensor('b2_bc', [128, D], F32, kind='ExternalInput').ap()
    ident_h = nc.dram_tensor('ident', [128, 128], F32, kind='ExternalInput').ap()
    cst_h = nc.dram_tensor('cst', [128, 4], F32, kind='ExternalInput').ap()

    qT_s = nc.dram_tensor('qT_s', [QC, 8, 128, 256], F32, kind='Internal').ap()
    x1_s = nc.dram_tensor('x1_s', [S, D], F32,
                          kind='ExternalInput' if 'x1in' in debug else 'Internal').ap()
    out_h = nc.dram_tensor('out', [S, D], F32, kind='ExternalOutput').ap()
    dbg = {}
    if 'p1' in debug:
        dbg['kT'] = nc.dram_tensor('dbg_kT', [128, 8 * 2048], F32, kind='ExternalOutput').ap()
        dbg['vst'] = nc.dram_tensor('dbg_vst', [128, NT * VW * H], F32, kind='ExternalOutput').ap()
        dbg['qT'] = nc.dram_tensor('dbg_qT', [QC, 8, 128, 256], F32, kind='ExternalOutput').ap()
    if 'p2' in debug:
        dbg['x1'] = nc.dram_tensor('dbg_x1', [S, D], F32, kind='ExternalOutput').ap()
    if 'p3a' in debug:
        dbg['ln2T'] = nc.dram_tensor('dbg_ln2T', [128, 8 * 2048], F32, kind='ExternalOutput').ap()
        dbg['fb'] = nc.dram_tensor('dbg_fb', [128, 16], F32, kind='ExternalOutput').ap()
    if 'p3b' in debug:
        dbg['yacc'] = nc.dram_tensor('dbg_yacc', [128, 16 * 1024], F32, kind='ExternalOutput').ap()

    from contextlib import ExitStack
    es = ExitStack()
    ident = es.enter_context(nc.sbuf_tensor('ident_sb', [128, 128], F32))
    cst = es.enter_context(nc.sbuf_tensor('cst_sb', [128, 4], F32))
    gam = es.enter_context(nc.sbuf_tensor('gam_sb', [128, 8], F32))
    gam1 = es.enter_context(nc.sbuf_tensor('gam1_sb', [128, 8], F32))
    es12 = ExitStack()
    kbuf = es12.enter_context(nc.sbuf_tensor('kbuf', [128, 8 * 2048], F32))
    vstore = es12.enter_context(nc.sbuf_tensor('vstore', [128, NT * VW * H], F32))

    sem_names = ['cV', 'cS', 'sTP', 'sTV', 'sMM', 'sS', 'sPV', 'sTT', 'sFT',
                 'sRT', 'sWM', 'sHM', 'sWM2', 'sBAR']
    sems = {name: Ctr(es.enter_context(nc.semaphore(name))) for name in sem_names}

    streams = {
        'const': DmaStream(nc, es, 'dCON', 1),
        'xld': DmaStream(nc, es, 'dX', 1),
        'w': DmaStream(nc, es, 'dW', 2),
        'qst': DmaStream(nc, es, 'dQS', 2),
        'ql': DmaStream(nc, es, 'dQL', 1),
        'xr2': DmaStream(nc, es, 'dXR', 1),
        'wo': DmaStream(nc, es, 'dWO', 1),
        'x1st': DmaStream(nc, es, 'dX1', 1),
        'a': DmaStream(nc, es, 'dA', 2),
        'w1': DmaStream(nc, es, 'dW1', 2),
        'w2': DmaStream(nc, es, 'dW2', 2),
        'ost': DmaStream(nc, es, 'dOS', 2),
        'dbg': DmaStream(nc, es, 'dDB', 1),
    }

    V = ChEng(nc.vector, sems['cV'])
    SC = ChEng(nc.scalar, sems['cS'])
    ctx = dict(nc=nc, sems=sems, streams=streams, V=V, SC=SC,
               kbuf=kbuf, vstore=vstore, ident=ident, cst=cst, gam=gam1)

    with es:
        tok_const = streams['const'].issue(
            nc.sync, [(ident[:], ident_h), (cst[:], cst_h), (gam[:], gamma_h)])
        vst_blk = vstore[:].rearrange('p (b w) -> p b w', w=VW)
        V.wait_tok(tok_const)
        V.op('memset', vst_blk[:, :, 64:65], 1.0)
        V.op('memset', vst_blk[:, :, 65:66], 0.0)
        vinit_tok = V.tok
        V.op('tensor_scalar', gam1[:], gam[:], 1.0, None, op0=ALU.add)
        for eng in (nc.tensor, nc.scalar, nc.sync, nc.gpsimd):
            wait_tok(eng, tok_const)
        nbar = 0

        if 1 in phases:
            _phase1(ctx, x_h, wqkv_h, qT_s, vinit_tok)
            nbar += 1
            _barrier(nc, sems['sBAR'], nbar, chains=(V, SC))
        if 2 in phases:
            _phase2(ctx, x_h, wout_h, qT_s, x1_s)
            nbar += 1
            _barrier(nc, sems['sBAR'], nbar, chains=(V, SC))
        if 3 in phases:
            es12.close()
            _phase3(ctx, x1_s, w1_h, b1_h, w2_h, b2_h, rw_h, out_h,
                    stages=[d for d in debug if d in ('p3a', 'p3b')], dbg=dbg)
            nbar += 1
            _barrier(nc, sems['sBAR'], nbar, chains=(V, SC))

        if 3 not in phases:
            es12.close()
        pairs = []
        if 'p1' in debug:
            pairs += [(dbg['kT'], kbuf[:]), (dbg['vst'], vstore[:]), (dbg['qT'], qT_s)]
        if 'p2' in debug:
            pairs += [(dbg['x1'], x1_s)]
        if pairs:
            for stn in ('qst', 'x1st'):
                for c in streams[stn].sems:
                    nc.gpsimd.wait_ge(c.sem, c.n)
            streams['dbg'].issue(nc.gpsimd, pairs)
        for stn in ('ost', 'dbg'):
            for c in streams[stn].sems:
                nc.gpsimd.wait_ge(c.sem, c.n)
    return nc


def _barrier(nc, bar, k, chains=()):
    for c in chains:
        c.e.wait_ge(c.c.sem, c.c.n)
    for eng in (nc.tensor, nc.vector, nc.scalar, nc.sync, nc.gpsimd):
        bar.inc(eng.nop())
    for eng in (nc.tensor, nc.vector, nc.scalar, nc.sync, nc.gpsimd):
        eng.wait_ge(bar.sem, 5 * k)


def _emit_stats(ctx, xtile, sqscr, st, rms):
    V, SC = ctx['V'], ctx['SC']
    t_sq = SC.op('activation', sqscr[:], xtile, AF.Square, accum_out=st[:, 1:2])
    V.op('reduce_sum', st[:, 0:1], xtile, axis=AX.X)
    V.wait_ge(SC.c.sem, t_sq)
    V.op('tensor_scalar', st[:, 2:3], st[:, 0:1], 1.0 / D, None, op0=ALU.mult)
    V.op('tensor_tensor', st[:, 3:4], st[:, 2:3], st[:, 2:3], op=ALU.mult)
    V.op('tensor_scalar', st[:, 4:5], st[:, 1:2], 1.0 / D, None, op0=ALU.mult)
    V.op('tensor_tensor', st[:, 5:6], st[:, 4:5], st[:, 3:4], op=ALU.subtract)
    V.op('tensor_scalar', st[:, 6:7], st[:, 5:6], 1e-5, None, op0=ALU.add)
    t_rv = V.op('reciprocal', st[:, 7:8], st[:, 6:7])
    if rms:
        V.op('tensor_scalar', st[:, 8:9], st[:, 5:6], float(D), None, op0=ALU.mult)
        t_ss2 = V.op('tensor_tensor', st[:, 8:9], st[:, 8:9], st[:, 7:8], op=ALU.mult)
        SC.wait_ge(V.c.sem, t_ss2)
        t_sqr = SC.op('activation', st[:, 12:14], st[:, 7:9], AF.Sqrt)
        V.wait_ge(SC.c.sem, t_sqr)
        V.op('tensor_scalar', st[:, 9:10], st[:, 13:14], 1e-12, None, op0=ALU.max)
        V.op('reciprocal', st[:, 9:10], st[:, 9:10])
        V.op('tensor_scalar', st[:, 10:11], st[:, 12:13], st[:, 9:10], 32.0,
             op0=ALU.mult, op1=ALU.mult)
    else:
        SC.wait_ge(V.c.sem, t_rv)
        t_sqr = SC.op('activation', st[:, 10:11], st[:, 7:8], AF.Sqrt)
        V.wait_ge(SC.c.sem, t_sqr)
    t_b = V.op('tensor_scalar', st[:, 11:12], st[:, 2:3], st[:, 10:11], -1.0,
               op0=ALU.mult, op1=ALU.mult)
    return t_b


def _phase1(ctx, x_h, wqkv_h, qT_s, vinit_tok):
    nc, sems, streams = ctx['nc'], ctx['sems'], ctx['streams']
    V, SC = ctx['V'], ctx['SC']
    kbuf, vstore, ident, cst, gam = (ctx[k] for k in ('kbuf', 'vstore', 'ident', 'cst', 'gam'))
    from contextlib import ExitStack
    es = ExitStack()
    xbuf = es.enter_context(nc.sbuf_tensor('p1_xbuf', [128, 4 * 1024], F32))
    xr = es.enter_context(nc.sbuf_tensor('p1_xr', [128, 2 * 1024], F32))
    sqscr = es.enter_context(nc.sbuf_tensor('p1_sqscr', [128, 1024], F32))
    st = es.enter_context(nc.sbuf_tensor('p1_st', [128, 16], F32))
    xtb = es.enter_context(nc.sbuf_tensor('p1_xtb', [128, 8 * 512], F32))
    wstrip = es.enter_context(nc.sbuf_tensor('p1_wstrip', [128, 2 * 1024], F32))
    qstage = es.enter_context(nc.sbuf_tensor('p1_qstage', [128, 2 * 512], F32))
    vtbuf = es.enter_context(nc.sbuf_tensor('p1_vtbuf', [128, 8 * 512], F32))
    ptp = es.enter_context(nc.psum_tensor('p1_ptp', [128, 2 * 512], F32))
    pqkv = es.enter_context(nc.psum_tensor('p1_pqkv', [128, 2 * 512], F32))
    pvt = es.enter_context(nc.psum_tensor('p1_pvt', [128, 2 * 512], F32))

    x_r = x_h.rearrange('(c t p) d -> c t p d', t=4, p=128)
    t_, y_, g_ = nc.tensor, nc.sync, nc.gpsimd
    strip_done = {}
    strip_ld = {}
    qst_tok = {}
    tok_xr = {}
    tok_xev = {}
    tok_xtb = {}
    tok_vcp = {}
    tok_vev = {}

    with es:
        for tc in range(NC4):
            if tc > 0:
                wait_tok(y_, (SC.c, tok_xr[4 * tc - 1]))
            xtok = streams['xld'].issue(
                y_, [(xbuf[:, t * 1024:(t + 1) * 1024], x_r[tc, t]) for t in range(4)])
            if tc == 0:
                for g in range(2):
                    strip_ld[g] = streams['w'].issue(
                        y_, [(wstrip[:, (g % 2) * 1024:(g % 2) * 1024 + 1024],
                              wqkv_h[g % 24].rearrange('p dp j -> p (dp j)'))])

            for t in range(4):
                idx = 4 * tc + t
                xtile = xbuf[:, t * 1024:(t + 1) * 1024]
                V.wait_tok(xtok)
                SC.wait_tok(xtok)
                _emit_stats(ctx, xtile, sqscr, st, rms=True)
                SC.wait_ge(V.c.sem, V.c.n)
                if idx >= 2:
                    SC.wait_ge(sems['sTP'].sem, 8 * (idx - 1))
                tok_xr[idx] = SC.op(
                    'activation', xr[:, (idx % 2) * 1024:(idx % 2) * 1024 + 1024],
                    xtile, AF.Identity, bias=st[:, 11:12], scale=st[:, 10:11])
                t_.wait_ge(SC.c.sem, tok_xr[idx])
                if t == 0 and tc > 0:
                    t_.wait_ge(sems['sMM'].sem, 24 * tc)
                if t == 0 and tc > 0:
                    V.wait_ge(sems['sMM'].sem, 24 * tc)
                for dp in range(8):
                    k = 8 * idx + dp
                    if k >= 2:
                        t_.wait_ge(V.c.sem, tok_xev[k - 2])
                    off = (k % 2) * 512 + ((k // 2) % 4) * 128
                    sems['sTP'].inc(
                        t_.transpose(ptp[:, off:off + 128],
                                     xr[:, (idx % 2) * 1024 + dp * 128:
                                        (idx % 2) * 1024 + dp * 128 + 128],
                                     ident[:]))
                    V.wait_ge(sems['sTP'].sem, sems['sTP'].n)
                    tok_xev[k] = V.op(
                        'tensor_scalar',
                        xtb[:, dp * 512 + t * 128: dp * 512 + t * 128 + 128],
                        ptp[:, off:off + 128],
                        gam[:, dp:dp + 1], None, op0=ALU.mult)
            tok_xtb[tc] = V.c.n

            for fp in range(24):
                g = 24 * tc + fp
                par = fp % 2
                wait_tok(t_, strip_ld[g])
                if fp == 0:
                    t_.wait_ge(V.c.sem, tok_xtb[tc])
                if g >= 2:
                    wait_tok(t_, strip_done[g - 2])
                for dp in range(8):
                    mm = t_.matmul(pqkv[:, par * 512:par * 512 + 512],
                                   wstrip[:, par * 1024 + dp * 128:
                                          par * 1024 + dp * 128 + 128],
                                   xtb[:, dp * 512:(dp + 1) * 512],
                                   start=(dp == 0), stop=(dp == 7))
                sems['sMM'].inc(mm)

                if fp < 8:
                    qg = 8 * tc + fp
                    SC.wait_ge(sems['sMM'].sem, sems['sMM'].n)
                    if qg >= 2:
                        SC.wait_tok(qst_tok[qg - 2])
                    tq = SC.op('activation',
                               qstage[:, (qg % 2) * 512:(qg % 2) * 512 + 512],
                               pqkv[:, par * 512:par * 512 + 512],
                               AF.Identity, scale=cst[:, C_QSCALE:C_QSCALE + 1],
                               bias=cst[:, C_ZERO:C_ZERO + 1])
                    strip_done[g] = (SC.c, tq)
                    g_.wait_ge(SC.c.sem, tq)
                    qst_tok[qg] = streams['qst'].issue(
                        g_, [(qT_s[2 * tc + half, fp],
                              qstage[:, (qg % 2) * 512 + half * 256:
                                     (qg % 2) * 512 + half * 256 + 256])
                             for half in range(2)])
                elif fp < 16:
                    pr = fp - 8
                    V.wait_ge(sems['sMM'].sem, sems['sMM'].n)
                    tk = V.op('tensor_copy',
                              kbuf[:, pr * 2048 + tc * 512: pr * 2048 + tc * 512 + 512],
                              pqkv[:, par * 512:par * 512 + 512])
                    strip_done[g] = (V.c, tk)
                else:
                    vp = fp - 16
                    V.wait_ge(sems['sMM'].sem, sems['sMM'].n)
                    if vp == 0 and tc > 0:
                        V.wait_ge(sems['sTV'].sem, 32 * tc)
                    tv = V.op('tensor_copy', vtbuf[:, vp * 512:(vp + 1) * 512],
                              pqkv[:, par * 512:par * 512 + 512])
                    strip_done[g] = (V.c, tv)
                gn = g + 2
                if gn < 24 * NC4:
                    wait_tok(y_, strip_done[g])
                    strip_ld[gn] = streams['w'].issue(
                        y_, [(wstrip[:, (gn % 2) * 1024:(gn % 2) * 1024 + 1024],
                              wqkv_h[gn % 24].rearrange('p dp j -> p (dp j)'))])
            tok_vev[tc] = V.c.n

            t_.wait_ge(V.c.sem, tok_vev[tc])
            for vp in range(8):
                for i in range(4):
                    k = (8 * tc + vp) * 4 + i
                    jt = 4 * tc + i
                    if k >= 2:
                        t_.wait_ge(V.c.sem, tok_vcp[k - 2])
                    off = (k % 2) * 512 + ((k // 2) % 4) * 128
                    sems['sTV'].inc(
                        t_.transpose(pvt[:, off:off + 128],
                                     vtbuf[:, vp * 512 + i * 128: vp * 512 + i * 128 + 128],
                                     ident[:]))
                    V.wait_ge(sems['sTV'].sem, sems['sTV'].n)
                    if k == 0:
                        V.wait_tok(vinit_tok)
                    for hh in range(2):
                        h = 2 * vp + hh
                        tok_vcp[k] = V.op(
                            'tensor_copy',
                            vstore[:, jt * VW * H + h * VW: jt * VW * H + h * VW + 64],
                            pvt[:, off + hh * 64: off + hh * 64 + 64])


def _phase2(ctx, x_h, wout_h, qT_s, x1_s):
    nc, sems, streams = ctx['nc'], ctx['sems'], ctx['streams']
    V, SC = ctx['V'], ctx['SC']
    kbuf, vstore, ident, cst = (ctx[k] for k in ('kbuf', 'vstore', 'ident', 'cst'))
    from contextlib import ExitStack
    es = ExitStack()
    woutb = es.enter_context(nc.sbuf_tensor('p2_wout', [128, 8 * 1024], F32))
    qbuf = es.enter_context(nc.sbuf_tensor('p2_qbuf', [128, 8 * 256], F32))
    xres = es.enter_context(nc.sbuf_tensor('p2_xres', [128, 2 * 1024], F32))
    ptb = es.enter_context(nc.sbuf_tensor('p2_ptb', [128, 2 * 256], F32))
    apv = es.enter_context(nc.sbuf_tensor('p2_apv', [128, 2 * 256], F32))
    atm = es.enter_context(nc.sbuf_tensor('p2_atm', [128, 2 * 1024], F32))
    rcb = es.enter_context(nc.sbuf_tensor('p2_rcb', [128, 4], F32))
    psS = [es.enter_context(nc.psum_tensor(f'p2_psS{i}', [128, 256], F32)) for i in range(2)]
    psPV = [es.enter_context(nc.psum_tensor(f'p2_psPV{i}', [128, 256], F32)) for i in range(2)]
    psT = [es.enter_context(nc.psum_tensor(f'p2_psT{i}', [128, 132], F32)) for i in range(2)]
    psY = [es.enter_context(nc.psum_tensor(f'p2_psY{i}', [128, 512], F32)) for i in range(2)]

    t_, y_, g_ = nc.tensor, nc.sync, nc.gpsimd
    x_r = x_h.rearrange('(c t p) d -> c t p d', t=2, p=128)
    x1_r = x1_s.rearrange('(c t p) d -> c t p d', t=2, p=128)

    tok_exp = {}
    tok_apv = {}
    tok_rc = {}
    tok_ae = {}
    tok_fe = {}
    tok_ye = {}
    x1_toks = []

    with es:
        wo_tok = streams['wo'].issue(
            y_, [(woutb[:, dp * 1024:(dp + 1) * 1024], wout_h[dp * 128:(dp + 1) * 128, :])
                 for dp in range(8)])
        for c in streams['qst'].sems:
            y_.wait_ge(c.sem, c.n)

        for qc in range(QC):
            if qc > 0:
                y_.wait_ge(sems['sWM'].sem, 4 * qc)
            q_tok = streams['ql'].issue(
                y_, [(qbuf[:, f * 256:(f + 1) * 256], qT_s[qc, f]) for f in range(8)])
            if qc > 0:
                wait_tok(y_, (V.c, tok_ye[4 * qc - 1]))
            x_tok = streams['xr2'].issue(
                y_, [(xres[:, t * 1024:(t + 1) * 1024], x_r[qc, t]) for t in range(2)])

            for h in range(H):
                hi = qc * H + h
                pr, hh = h // 2, h % 2
                base = hh * 64
                qslice = qbuf[base:base + 64, pr * 256:(pr + 1) * 256]
                if h == 0:
                    wait_tok(t_, q_tok)
                for j in range(16):
                    k = hi * 16 + j
                    if k >= 2:
                        t_.wait_ge(SC.c.sem, tok_exp[k - 2])
                    sems['sS'].inc(
                        t_.matmul(psS[k % 2][:, :],
                                  kbuf[base:base + 64,
                                       pr * 2048 + j * 128: pr * 2048 + j * 128 + 128],
                                  qslice, start=True, stop=True))
                    SC.wait_ge(sems['sS'].sem, k + 1)
                    if k >= 2:
                        SC.wait_ge(sems['sPV'].sem, k - 1)
                    tok_exp[k] = SC.op(
                        'activation', ptb[:, (k % 2) * 256:(k % 2) * 256 + 256],
                        psS[k % 2][:, :], AF.Exp, bias=cst[:, C_NSHIFT:C_NSHIFT + 1])
                    t_.wait_ge(SC.c.sem, tok_exp[k])
                    if j == 0 and hi >= 2:
                        wait_tok(t_, (V.c, tok_apv[hi - 2]))
                    sems['sPV'].inc(
                        t_.matmul(psPV[hi % 2][0:VW, :],
                                  vstore[:, j * VW * H + h * VW: j * VW * H + h * VW + VW],
                                  ptb[:, (k % 2) * 256:(k % 2) * 256 + 256],
                                  start=(j == 0), stop=(j == 15)))

                V.wait_ge(sems['sPV'].sem, 16 * (hi + 1))
                if hi >= 2:
                    V.wait_ge(sems['sTT'].sem, 2 * (hi - 1))
                tok_apv[hi] = V.op(
                    'tensor_copy', apv[0:VW, (hi % 2) * 256:(hi % 2) * 256 + 256],
                    psPV[hi % 2][0:VW, :])
                t_.wait_ge(V.c.sem, tok_apv[hi])
                if hi >= 2:
                    t_.wait_ge(SC.c.sem, tok_ae[hi - 2])
                for tp in range(2):
                    sems['sTT'].inc(
                        t_.transpose(psT[hi % 2][:, tp * 66:tp * 66 + 66],
                                     apv[0:VW, (hi % 2) * 256 + tp * 128:
                                         (hi % 2) * 256 + tp * 128 + 128],
                                     ident[0:VW, 0:VW]))
                V.wait_ge(sems['sTT'].sem, 2 * (hi + 1))
                for tp in range(2):
                    tok_rc[hi] = V.op(
                        'reciprocal', rcb[:, (hi % 2) * 2 + tp:(hi % 2) * 2 + tp + 1],
                        psT[hi % 2][:, tp * 66 + 64: tp * 66 + 65])
                SC.wait_ge(V.c.sem, tok_rc[hi])
                if h == 0 and qc > 0:
                    wait_tok(SC, x1_toks[qc - 1])
                for tp in range(2):
                    tok_ae[hi] = SC.op(
                        'activation', atm[:, tp * 1024 + h * 64: tp * 1024 + h * 64 + 64],
                        psT[hi % 2][:, tp * 66: tp * 66 + 64],
                        AF.Identity,
                        scale=rcb[:, (hi % 2) * 2 + tp:(hi % 2) * 2 + tp + 1],
                        bias=cst[:, C_ZERO:C_ZERO + 1])

            t_.wait_ge(SC.c.sem, tok_ae[qc * H + H - 1])
            for tp in range(2):
                for dp in range(8):
                    k = qc * 16 + tp * 8 + dp
                    if k >= 2:
                        wait_tok(t_, (V.c, tok_fe[k - 2]))
                    bank, half = k % 2, (k // 2) % 2
                    sems['sFT'].inc(
                        t_.transpose(psS[bank][:, half * 128:half * 128 + 128],
                                     atm[:, tp * 1024 + dp * 128: tp * 1024 + dp * 128 + 128],
                                     ident[:]))
                    V.wait_ge(sems['sFT'].sem, sems['sFT'].n)
                    tok_fe[k] = V.op(
                        'tensor_copy', qbuf[:, dp * 256 + tp * 128: dp * 256 + tp * 128 + 128],
                        psS[bank][:, half * 128:half * 128 + 128])

            wait_tok(t_, (V.c, tok_fe[qc * 16 + 15]))
            if qc == 0:
                wait_tok(t_, wo_tok)
            for tp in range(2):
                for nh in range(2):
                    u = 4 * qc + 2 * tp + nh
                    if u >= 2:
                        wait_tok(t_, (V.c, tok_ye[u - 2]))
                    for dp in range(8):
                        mm = t_.matmul(psY[nh][:, :],
                                       qbuf[:, dp * 256 + tp * 128: dp * 256 + tp * 128 + 128],
                                       woutb[:, dp * 1024 + nh * 512: dp * 1024 + nh * 512 + 512],
                                       start=(dp == 0), stop=(dp == 7))
                    sems['sWM'].inc(mm)
                    V.wait_ge(sems['sWM'].sem, sems['sWM'].n)
                    if tp == 0 and nh == 0:
                        V.wait_tok(x_tok)
                    tok_ye[u] = V.op(
                        'tensor_tensor',
                        atm[:, tp * 1024 + nh * 512: tp * 1024 + nh * 512 + 512],
                        xres[:, tp * 1024 + nh * 512: tp * 1024 + nh * 512 + 512],
                        psY[nh][:, :], op=ALU.add)
            g_.wait_ge(V.c.sem, tok_ye[4 * qc + 3])
            x1_toks.append(streams['x1st'].issue(
                g_, [(x1_r[qc, t], atm[:, t * 1024:(t + 1) * 1024]) for t in range(2)]))


def _phase3(ctx, x1_s, w1_h, b1_h, w2_h, b2_h, rw_h, out_h, stages=(), dbg=None):
    nc, sems, streams = ctx['nc'], ctx['sems'], ctx['streams']
    V, SC = ctx['V'], ctx['SC']
    ident, cst = ctx['ident'], ctx['cst']
    from contextlib import ExitStack
    es = ExitStack()
    ln2T = es.enter_context(nc.sbuf_tensor('p3_ln2T', [128, 8 * 2048], F32))
    yacc = es.enter_context(nc.sbuf_tensor('p3_yacc', [128, 16 * 1024], F32))
    x1t = es.enter_context(nc.sbuf_tensor('p3_x1t', [128, 2 * 1024], F32))
    ln2 = es.enter_context(nc.sbuf_tensor('p3_ln2', [128, 2 * 1024], F32))
    sqscr = es.enter_context(nc.sbuf_tensor('p3_sqscr', [128, 1024], F32))
    st = es.enter_context(nc.sbuf_tensor('p3_st', [128, 16], F32))
    rwb = es.enter_context(nc.sbuf_tensor('p3_rw', [128, 64], F32))
    rtb = es.enter_context(nc.sbuf_tensor('p3_rt', [128, 8], F32))
    rpb = es.enter_context(nc.sbuf_tensor('p3_rp', [128, 24], F32))
    fbuf = es.enter_context(nc.sbuf_tensor('p3_fb', [128, 16], F32))
    b1c = es.enter_context(nc.sbuf_tensor('p3_b1c', [128, 32], F32))
    b2b = es.enter_context(nc.sbuf_tensor('p3_b2b', [128, 1024], F32))
    w1s = es.enter_context(nc.sbuf_tensor('p3_w1s', [128, 2 * 1024], F32))
    w2s = es.enter_context(nc.sbuf_tensor('p3_w2s', [128, 2 * 1024], F32))
    hTb = es.enter_context(nc.sbuf_tensor('p3_hT', [128, 2 * 2048], F32))
    tmpb = es.enter_context(nc.sbuf_tensor('p3_tmp', [128, 1024], F32))
    outb = es.enter_context(nc.sbuf_tensor('p3_out', [128, 2 * 1024], F32))

    t_, y_, g_ = nc.tensor, nc.sync, nc.gpsimd
    x1_r = x1_s.rearrange('(t p) d -> t p d', p=128)
    out_r = out_h.rearrange('(t p) d -> t p d', p=128)
    tp_base = sems['sTP'].n

    tok_ln = {}
    tok_le = {}
    tok_rexp = {}
    tok_fc = {}
    tok_ge = {}
    tok_ya = {}
    ost_toks = {}

    with es:
        small_tok = streams['const'].issue(
            y_, [(rwb[:, dp * 8:(dp + 1) * 8], rw_h[dp * 128:(dp + 1) * 128, :])
                 for dp in range(8)] + [(b1c[:], b1_h), (b2b[:], b2_h)])
        for c in streams['x1st'].sems:
            y_.wait_ge(c.sem, c.n)

        esA = ExitStack()
        ptr = esA.enter_context(nc.psum_tensor('p3_ptr', [128, 2 * 512], F32))
        plog = esA.enter_context(nc.psum_tensor('p3_plog', [128, 8], F32))
        with esA:
            for t in range(NT):
                if t >= 2:
                    wait_tok(y_, (SC.c, tok_ln[t - 2]))
                a_tok = streams['a'].issue(
                    y_, [(x1t[:, (t % 2) * 1024:(t % 2) * 1024 + 1024], x1_r[t])])
                xtile = x1t[:, (t % 2) * 1024:(t % 2) * 1024 + 1024]
                V.wait_tok(a_tok)
                SC.wait_tok(a_tok)
                _emit_stats(ctx, xtile, sqscr, st, rms=False)
                SC.wait_ge(V.c.sem, V.c.n)
                if t >= 2:
                    SC.wait_ge(sems['sTP'].sem, tp_base + 8 * (t - 1))
                tok_ln[t] = SC.op(
                    'activation', ln2[:, (t % 2) * 1024:(t % 2) * 1024 + 1024], xtile,
                    AF.Identity, bias=st[:, 11:12], scale=st[:, 10:11])
                t_.wait_ge(SC.c.sem, tok_ln[t])
                for dp in range(8):
                    k = 8 * t + dp
                    if k >= 2:
                        wait_tok(t_, (V.c, tok_le[k - 2]))
                    off = (k % 2) * 512 + ((k // 2) % 4) * 128
                    sems['sTP'].inc(
                        t_.transpose(ptr[:, off:off + 128],
                                     ln2[:, (t % 2) * 1024 + dp * 128:
                                         (t % 2) * 1024 + dp * 128 + 128],
                                     ident[:]))
                    V.wait_ge(sems['sTP'].sem, sems['sTP'].n)
                    tok_le[k] = V.op(
                        'tensor_copy', ln2T[:, dp * 2048 + t * 128: dp * 2048 + t * 128 + 128],
                        ptr[:, off:off + 128])
                wait_tok(t_, (V.c, tok_le[8 * t + 7]))
                if t == 0:
                    wait_tok(t_, small_tok)
                if t >= 1:
                    t_.wait_ge(SC.c.sem, tok_rexp[t - 1])
                for dp in range(8):
                    mm = t_.matmul(plog[:, :],
                                   ln2T[:, dp * 2048 + t * 128: dp * 2048 + t * 128 + 128],
                                   rwb[:, dp * 8:(dp + 1) * 8],
                                   start=(dp == 0), stop=(dp == 7))
                sems['sRT'].inc(mm)
                V.wait_ge(sems['sRT'].sem, sems['sRT'].n)
                V.op('reduce_max', rtb[:, 0:1], plog[:, :], axis=AX.X)
                t_nm = V.op('tensor_scalar', rtb[:, 1:2], rtb[:, 0:1], -1.0, None,
                            op0=ALU.mult)
                SC.wait_ge(V.c.sem, t_nm)
                tok_rexp[t] = SC.op('activation', rpb[:, 0:8], plog[:, :], AF.Exp,
                                    bias=rtb[:, 1:2])
                V.wait_ge(SC.c.sem, tok_rexp[t])
                V.op('reduce_sum', rtb[:, 2:3], rpb[:, 0:8], axis=AX.X)
                V.op('reciprocal', rtb[:, 3:4], rtb[:, 2:3])
                V.op('tensor_scalar', rpb[:, 8:16], rpb[:, 0:8], rtb[:, 3:4], None,
                     op0=ALU.mult)
                V.op('reduce_max', rtb[:, 4:5], rpb[:, 8:16], axis=AX.X)
                V.op('tensor_scalar', rpb[:, 16:24], rpb[:, 8:16], rtb[:, 4:5], None,
                     op0=ALU.is_equal)
                V.op('scalar_tensor_tensor', rpb[:, 16:24], rpb[:, 16:24], -1e30,
                     rpb[:, 8:16], op0=ALU.mult, op1=ALU.add)
                V.op('reduce_max', rtb[:, 5:6], rpb[:, 16:24], axis=AX.X)
                V.op('tensor_tensor', rtb[:, 6:7], rtb[:, 4:5], rtb[:, 5:6], op=ALU.add)
                V.op('reciprocal', rtb[:, 7:8], rtb[:, 6:7])
                tok_fc[t] = V.op('tensor_tensor', fbuf[:, t:t + 1], rtb[:, 6:7],
                                 rtb[:, 7:8], op=ALU.mult)
        hsS_afterA = SC.c.n
        tok_lnT_all = tok_le[8 * NT - 8 + 7]
        if 'p3a' in stages:
            g_.wait_ge(V.c.sem, V.c.n)
            streams['dbg'].issue(g_, [(dbg['ln2T'], ln2T[:]), (dbg['fb'], fbuf[:])])
            return

        ph = [es.enter_context(nc.psum_tensor(f'p3_ph{i}', [128, 512], F32)) for i in range(2)]
        py = [es.enter_context(nc.psum_tensor(f'p3_py{i}', [128, 1024], F32)) for i in range(2)]
        w1_toks, w2_toks = {}, {}
        for fg in range(16):
            for fp2 in range(2):
                fp = 2 * fg + fp2
                if fp >= 2:
                    y_.wait_ge(sems['sHM'].sem, 4 * (fp - 1))
                w1_toks[fp] = streams['w1'].issue(
                    y_, [(w1s[:, (fp % 2) * 1024:(fp % 2) * 1024 + 1024],
                          w1_h[fp].rearrange('p dp j -> p (dp j)'))])
                if fg >= 1:
                    y_.wait_ge(sems['sWM2'].sem, 16 * fg)
                w2_toks[fp] = streams['w2'].issue(
                    y_, [(w2s[:, fp2 * 1024:fp2 * 1024 + 1024],
                          w2_h[fp * 128:(fp + 1) * 128, :])])
                for tc4 in range(4):
                    m = 4 * fp + tc4
                    wait_tok(t_, w1_toks[fp])
                    if fp == 0 and tc4 == 0:
                        wait_tok(t_, (V.c, tok_lnT_all))
                        t_.wait_ge(SC.c.sem, hsS_afterA)
                    if m >= 2:
                        wait_tok(t_, (SC.c, tok_ge[m - 2]))
                    for dp in range(8):
                        mm = t_.matmul(ph[m % 2][:, :],
                                       w1s[:, (fp % 2) * 1024 + dp * 128:
                                           (fp % 2) * 1024 + dp * 128 + 128],
                                       ln2T[:, dp * 2048 + tc4 * 512:
                                            dp * 2048 + tc4 * 512 + 512],
                                       start=(dp == 0), stop=(dp == 7))
                    sems['sHM'].inc(mm)
                    SC.wait_ge(sems['sHM'].sem, sems['sHM'].n)
                    if fg >= 1 and tc4 == 0 and fp2 == 0:
                        SC.wait_ge(sems['sWM2'].sem, 16 * fg)
                    tok_ge[m] = SC.op(
                        'activation', hTb[:, fp2 * 2048 + tc4 * 512:
                                          fp2 * 2048 + tc4 * 512 + 512],
                        ph[m % 2][:, :], AF.Gelu_apprx_tanh, bias=b1c[:, fp:fp + 1])
            for t16 in range(NT):
                w = 16 * fg + t16
                t_.wait_ge(SC.c.sem, tok_ge[4 * (2 * fg + 1) + 3])
                wait_tok(t_, w2_toks[2 * fg])
                wait_tok(t_, w2_toks[2 * fg + 1])
                if w >= 2:
                    wait_tok(t_, (V.c, tok_ya[w - 2]))
                for nh in range(2):
                    for fp2 in range(2):
                        mm = t_.matmul(py[w % 2][:, nh * 512:nh * 512 + 512],
                                       hTb[:, fp2 * 2048 + t16 * 128:
                                           fp2 * 2048 + t16 * 128 + 128],
                                       w2s[:, fp2 * 1024 + nh * 512:
                                           fp2 * 1024 + nh * 512 + 512],
                                       start=(fp2 == 0), stop=(fp2 == 1))
                sems['sWM2'].inc(mm)
                V.wait_ge(sems['sWM2'].sem, sems['sWM2'].n)
                if fg == 0:
                    tok_ya[w] = V.op('tensor_copy', yacc[:, t16 * 1024:(t16 + 1) * 1024],
                                     py[w % 2][:, :])
                else:
                    tok_ya[w] = V.op('tensor_tensor', yacc[:, t16 * 1024:(t16 + 1) * 1024],
                                     yacc[:, t16 * 1024:(t16 + 1) * 1024],
                                     py[w % 2][:, :], op=ALU.add)

        if 'p3b' in stages:
            g_.wait_ge(V.c.sem, V.c.n)
            streams['dbg'].issue(g_, [(dbg['yacc'], yacc[:])])
            return

        tok_oadd = {}
        tok_osc = {}
        for t in range(NT):
            if t >= 2:
                wait_tok(y_, (V.c, tok_oadd[t - 2]))
            c_tok = streams['a'].issue(
                y_, [(x1t[:, (t % 2) * 1024:(t % 2) * 1024 + 1024], x1_r[t])])
            if t >= 1:
                V.wait_ge(SC.c.sem, tok_osc[t - 1])
            t_tmp = V.op('tensor_tensor', tmpb[:], yacc[:, t * 1024:(t + 1) * 1024],
                         b2b[:], op=ALU.add)
            SC.wait_ge(V.c.sem, t_tmp)
            SC.wait_ge(V.c.sem, tok_fc[NT - 1])
            if t >= 2:
                wait_tok(SC, ost_toks[t - 2])
            tok_osc[t] = SC.op(
                'activation', outb[:, (t % 2) * 1024:(t % 2) * 1024 + 1024], tmpb[:],
                AF.Identity, scale=fbuf[:, t:t + 1], bias=cst[:, C_ZERO:C_ZERO + 1])
            V.wait_ge(SC.c.sem, tok_osc[t])
            V.wait_tok(c_tok)
            tok_oadd[t] = V.op(
                'tensor_tensor', outb[:, (t % 2) * 1024:(t % 2) * 1024 + 1024],
                outb[:, (t % 2) * 1024:(t % 2) * 1024 + 1024],
                x1t[:, (t % 2) * 1024:(t % 2) * 1024 + 1024], op=ALU.add)
            g_.wait_ge(V.c.sem, tok_oadd[t])
            ost_toks[t] = streams['ost'].issue(
                g_, [(out_r[t], outb[:, (t % 2) * 1024:(t % 2) * 1024 + 1024])])


_NC_CACHE = {}


def kernel(L0_attn_x, L0_attn_gamma, L0_attn_w_qkv, L0_attn_w_out,
           L0_ffn_router_w, L0_ffn_w1, L0_ffn_b1, L0_ffn_w2, L0_ffn_b2):
    from concourse.bass_utils import run_bass_kernel_spmd
    x_full = np.asarray(L0_attn_x, np.float32)
    n = 8
    base = host_inputs(x_full[0], L0_attn_gamma, L0_attn_w_qkv, L0_attn_w_out,
                       L0_ffn_router_w, L0_ffn_w1, L0_ffn_b1, L0_ffn_w2, L0_ffn_b2)
    in_maps = []
    for b in range(n):
        m = dict(base)
        m['x'] = np.ascontiguousarray(x_full[b])
        in_maps.append(m)
    if 'nc' not in _NC_CACHE:
        _NC_CACHE['nc'] = build()
    res = run_bass_kernel_spmd(_NC_CACHE['nc'], in_maps, core_ids=list(range(n)))
    out = np.stack([np.asarray(r['out'], np.float32) for r in res.results], axis=0)
    return out


# version 02
# speedup vs baseline: 1.2899x; 1.2899x over previous
import numpy as np
import concourse.bass as bass
import concourse.mybir as mybir

F32 = mybir.dt.float32
AF = mybir.ActivationFunctionType
ALU = mybir.AluOpType
AX = mybir.AxisListType

S = 2048
D = 1024
H = 16
F = 4096
NT = S // 128
NC4 = 4
QC = 8
VW = 66
SHIFT = 20.0
C_NSHIFT, C_QSCALE, C_ONE, C_ZERO = 0, 1, 2, 3


def host_inputs(x, gamma, wqkv, wout, router_w, w1, b1, w2, b2):
    d = {}
    d['x'] = np.ascontiguousarray(np.asarray(x, np.float32))
    d['gamma_c'] = np.ascontiguousarray(np.asarray(gamma, np.float32).reshape(8, 128).T)
    d['wqkv_bl'] = np.ascontiguousarray(
        np.asarray(wqkv, np.float32).reshape(8, 128, 24, 128).transpose(2, 1, 0, 3))
    d['wout'] = np.ascontiguousarray(np.asarray(wout, np.float32))
    d['rw'] = np.ascontiguousarray(np.asarray(router_w, np.float32))
    d['w1_bl'] = np.ascontiguousarray(
        np.asarray(w1, np.float32).reshape(8, 128, 32, 128).transpose(2, 1, 0, 3))
    d['b1_c'] = np.ascontiguousarray(np.asarray(b1, np.float32).reshape(32, 128).T)
    d['w2'] = np.ascontiguousarray(np.asarray(w2, np.float32))
    d['b2_bc'] = np.ascontiguousarray(np.broadcast_to(np.asarray(b2, np.float32), (128, D)))
    d['ident'] = np.eye(128, dtype=np.float32)
    cst = np.zeros((128, 4), np.float32)
    cst[:, C_NSHIFT] = -SHIFT
    cst[:, C_QSCALE] = 0.125
    cst[:, C_ONE] = 1.0
    d['cst'] = cst
    return d


class Ctr:
    def __init__(self, sem):
        self.sem = sem
        self.n = 0

    def inc(self, inst, by=1):
        inst.then_inc(self.sem, by)
        self.n += by
        return self.n

    def wait(self, eng, target=None):
        eng.wait_ge(self.sem, self.n if target is None else target)


class ChEng:

    def __init__(self, eng, ctr):
        self.e = eng
        self.c = ctr

    def op(self, name, *a, **k):
        self.e.wait_ge(self.c.sem, self.c.n)
        inst = getattr(self.e, name)(*a, **k)
        self.c.inc(inst)
        return self.c.n

    def wait_ge(self, sem, n):
        self.e.wait_ge(sem, n)

    def wait_tok(self, tok):
        c, n = tok
        self.e.wait_ge(c.sem, n)

    @property
    def tok(self):
        return (self.c, self.c.n)


class DmaStream:

    def __init__(self, nc, es, name, nbuf=1):
        self.sems = [Ctr(es.enter_context(nc.semaphore(f'{name}{i}'))) for i in range(nbuf)]
        self.nbuf = nbuf
        self.i = 0

    def issue(self, eng, pairs):
        c = self.sems[self.i % self.nbuf]
        eng.wait_ge(c.sem, c.n)
        for out, in_ in pairs:
            c.inc(eng.dma_start(out, in_), 16)
        self.i += 1
        return (c, c.n)


def wait_tok(eng, tok):
    c, n = tok
    eng.wait_ge(c.sem, n)


def build():
    phases = (1, 2, 3)
    debug = ()
    nc = bass.Bass('TRN2', target_bir_lowering=False)

    x_h = nc.dram_tensor('x', [S, D], F32, kind='ExternalInput').ap()
    gamma_h = nc.dram_tensor('gamma_c', [128, 8], F32, kind='ExternalInput').ap()
    wqkv_h = nc.dram_tensor('wqkv_bl', [24, 128, 8, 128], F32, kind='ExternalInput').ap()
    wout_h = nc.dram_tensor('wout', [D, D], F32, kind='ExternalInput').ap()
    rw_h = nc.dram_tensor('rw', [D, 8], F32, kind='ExternalInput').ap()
    w1_h = nc.dram_tensor('w1_bl', [32, 128, 8, 128], F32, kind='ExternalInput').ap()
    b1_h = nc.dram_tensor('b1_c', [128, 32], F32, kind='ExternalInput').ap()
    w2_h = nc.dram_tensor('w2', [F, D], F32, kind='ExternalInput').ap()
    b2_h = nc.dram_tensor('b2_bc', [128, D], F32, kind='ExternalInput').ap()
    ident_h = nc.dram_tensor('ident', [128, 128], F32, kind='ExternalInput').ap()
    cst_h = nc.dram_tensor('cst', [128, 4], F32, kind='ExternalInput').ap()

    qT_s = nc.dram_tensor('qT_s', [QC, 8, 128, 256], F32, kind='Internal').ap()
    x1_s = nc.dram_tensor('x1_s', [S, D], F32,
                          kind='ExternalInput' if 'x1in' in debug else 'Internal').ap()
    out_h = nc.dram_tensor('out', [S, D], F32, kind='ExternalOutput').ap()
    dbg = {}
    if 'p1' in debug:
        dbg['kT'] = nc.dram_tensor('dbg_kT', [128, 8 * 2048], F32, kind='ExternalOutput').ap()
        dbg['vst'] = nc.dram_tensor('dbg_vst', [128, NT * VW * H], F32, kind='ExternalOutput').ap()
        dbg['qT'] = nc.dram_tensor('dbg_qT', [QC, 8, 128, 256], F32, kind='ExternalOutput').ap()
    if 'p2' in debug:
        dbg['x1'] = nc.dram_tensor('dbg_x1', [S, D], F32, kind='ExternalOutput').ap()
    if 'p3a' in debug:
        dbg['ln2T'] = nc.dram_tensor('dbg_ln2T', [128, 8 * 2048], F32, kind='ExternalOutput').ap()
        dbg['fb'] = nc.dram_tensor('dbg_fb', [128, 16], F32, kind='ExternalOutput').ap()
    if 'p3b' in debug:
        dbg['yacc'] = nc.dram_tensor('dbg_yacc', [128, 16 * 1024], F32, kind='ExternalOutput').ap()

    from contextlib import ExitStack
    es = ExitStack()
    ident = es.enter_context(nc.sbuf_tensor('ident_sb', [128, 128], F32))
    cst = es.enter_context(nc.sbuf_tensor('cst_sb', [128, 4], F32))
    gam = es.enter_context(nc.sbuf_tensor('gam_sb', [128, 8], F32))
    gam1 = es.enter_context(nc.sbuf_tensor('gam1_sb', [128, 8], F32))
    es12 = ExitStack()
    kbuf = es12.enter_context(nc.sbuf_tensor('kbuf', [128, 8 * 2048], F32))
    vstore = es12.enter_context(nc.sbuf_tensor('vstore', [128, NT * VW * H], F32))

    sem_names = ['cV', 'cS', 'sTP', 'sTV', 'sMM', 'sS', 'sPV', 'sTT', 'sFT',
                 'sRT', 'sWM', 'sHM', 'sWM2', 'sBAR']
    sems = {name: Ctr(es.enter_context(nc.semaphore(name))) for name in sem_names}

    streams = {
        'const': DmaStream(nc, es, 'dCON', 1),
        'xld': DmaStream(nc, es, 'dX', 1),
        'w': DmaStream(nc, es, 'dW', 2),
        'qst': DmaStream(nc, es, 'dQS', 2),
        'ql': DmaStream(nc, es, 'dQL', 1),
        'xr2': DmaStream(nc, es, 'dXR', 1),
        'wo': DmaStream(nc, es, 'dWO', 1),
        'x1st': DmaStream(nc, es, 'dX1', 1),
        'a': DmaStream(nc, es, 'dA', 2),
        'w1': DmaStream(nc, es, 'dW1', 2),
        'w2': DmaStream(nc, es, 'dW2', 2),
        'ost': DmaStream(nc, es, 'dOS', 2),
        'dbg': DmaStream(nc, es, 'dDB', 1),
    }

    V = ChEng(nc.vector, sems['cV'])
    SC = ChEng(nc.scalar, sems['cS'])
    ctx = dict(nc=nc, sems=sems, streams=streams, V=V, SC=SC,
               kbuf=kbuf, vstore=vstore, ident=ident, cst=cst, gam=gam1)

    with es:
        tok_const = streams['const'].issue(
            nc.sync, [(ident[:], ident_h), (cst[:], cst_h), (gam[:], gamma_h)])
        vst_blk = vstore[:].rearrange('p (b w) -> p b w', w=VW)
        V.wait_tok(tok_const)
        V.op('memset', vst_blk[:, :, 64:65], 1.0)
        V.op('memset', vst_blk[:, :, 65:66], 0.0)
        vinit_tok = V.tok
        V.op('tensor_scalar', gam1[:], gam[:], 1.0, None, op0=ALU.add)
        for eng in (nc.tensor, nc.scalar, nc.sync, nc.gpsimd):
            wait_tok(eng, tok_const)
        nbar = 0

        if 1 in phases:
            _phase1(ctx, x_h, wqkv_h, qT_s, vinit_tok)
            nbar += 1
            _barrier(nc, sems['sBAR'], nbar, chains=(V, SC))
        if 2 in phases:
            _phase2(ctx, x_h, wout_h, qT_s, x1_s)
            nbar += 1
            _barrier(nc, sems['sBAR'], nbar, chains=(V, SC))
        if 3 in phases:
            es12.close()
            _phase3(ctx, x1_s, w1_h, b1_h, w2_h, b2_h, rw_h, out_h,
                    stages=[d for d in debug if d in ('p3a', 'p3b')], dbg=dbg)
            nbar += 1
            _barrier(nc, sems['sBAR'], nbar, chains=(V, SC))

        if 3 not in phases:
            es12.close()
        pairs = []
        if 'p1' in debug:
            pairs += [(dbg['kT'], kbuf[:]), (dbg['vst'], vstore[:]), (dbg['qT'], qT_s)]
        if 'p2' in debug:
            pairs += [(dbg['x1'], x1_s)]
        if pairs:
            for stn in ('qst', 'x1st'):
                for c in streams[stn].sems:
                    nc.gpsimd.wait_ge(c.sem, c.n)
            streams['dbg'].issue(nc.gpsimd, pairs)
        for stn in ('ost', 'dbg'):
            for c in streams[stn].sems:
                nc.gpsimd.wait_ge(c.sem, c.n)
    return nc


def _barrier(nc, bar, k, chains=()):
    for c in chains:
        c.e.wait_ge(c.c.sem, c.c.n)
    for eng in (nc.tensor, nc.vector, nc.scalar, nc.sync, nc.gpsimd):
        bar.inc(eng.nop())
    for eng in (nc.tensor, nc.vector, nc.scalar, nc.sync, nc.gpsimd):
        eng.wait_ge(bar.sem, 5 * k)


def _emit_stats(ctx, xtile, sqscr, st, rms):
    V, SC = ctx['V'], ctx['SC']
    t_sq = SC.op('activation', sqscr[:], xtile, AF.Square, accum_out=st[:, 1:2])
    V.op('reduce_sum', st[:, 0:1], xtile, axis=AX.X)
    V.wait_ge(SC.c.sem, t_sq)
    V.op('tensor_scalar', st[:, 2:3], st[:, 0:1], 1.0 / D, None, op0=ALU.mult)
    V.op('tensor_tensor', st[:, 3:4], st[:, 2:3], st[:, 2:3], op=ALU.mult)
    V.op('tensor_scalar', st[:, 4:5], st[:, 1:2], 1.0 / D, None, op0=ALU.mult)
    V.op('tensor_tensor', st[:, 5:6], st[:, 4:5], st[:, 3:4], op=ALU.subtract)
    V.op('tensor_scalar', st[:, 6:7], st[:, 5:6], 1e-5, None, op0=ALU.add)
    t_rv = V.op('reciprocal', st[:, 7:8], st[:, 6:7])
    if rms:
        V.op('tensor_scalar', st[:, 8:9], st[:, 5:6], float(D), None, op0=ALU.mult)
        t_ss2 = V.op('tensor_tensor', st[:, 8:9], st[:, 8:9], st[:, 7:8], op=ALU.mult)
        SC.wait_ge(V.c.sem, t_ss2)
        t_sqr = SC.op('activation', st[:, 12:14], st[:, 7:9], AF.Sqrt)
        V.wait_ge(SC.c.sem, t_sqr)
        V.op('tensor_scalar', st[:, 9:10], st[:, 13:14], 1e-12, None, op0=ALU.max)
        V.op('reciprocal', st[:, 9:10], st[:, 9:10])
        V.op('tensor_scalar', st[:, 10:11], st[:, 12:13], st[:, 9:10], 32.0,
             op0=ALU.mult, op1=ALU.mult)
    else:
        SC.wait_ge(V.c.sem, t_rv)
        t_sqr = SC.op('activation', st[:, 10:11], st[:, 7:8], AF.Sqrt)
        V.wait_ge(SC.c.sem, t_sqr)
    t_b = V.op('tensor_scalar', st[:, 11:12], st[:, 2:3], st[:, 10:11], -1.0,
               op0=ALU.mult, op1=ALU.mult)
    return t_b


def _phase1(ctx, x_h, wqkv_h, qT_s, vinit_tok):
    nc, sems, streams = ctx['nc'], ctx['sems'], ctx['streams']
    V, SC = ctx['V'], ctx['SC']
    kbuf, vstore, ident, cst, gam = (ctx[k] for k in ('kbuf', 'vstore', 'ident', 'cst', 'gam'))
    from contextlib import ExitStack
    es = ExitStack()
    xbuf = es.enter_context(nc.sbuf_tensor('p1_xbuf', [128, 4 * 1024], F32))
    xr = es.enter_context(nc.sbuf_tensor('p1_xr', [128, 2 * 1024], F32))
    sqscr = es.enter_context(nc.sbuf_tensor('p1_sqscr', [128, 1024], F32))
    st = es.enter_context(nc.sbuf_tensor('p1_st', [128, 16], F32))
    xtb = es.enter_context(nc.sbuf_tensor('p1_xtb', [128, 8 * 512], F32))
    wstrip = es.enter_context(nc.sbuf_tensor('p1_wstrip', [128, 2 * 1024], F32))
    qstage = es.enter_context(nc.sbuf_tensor('p1_qstage', [128, 2 * 512], F32))
    vtbuf = es.enter_context(nc.sbuf_tensor('p1_vtbuf', [128, 8 * 512], F32))
    ptp = es.enter_context(nc.psum_tensor('p1_ptp', [128, 2 * 512], F32))
    pqkv = es.enter_context(nc.psum_tensor('p1_pqkv', [128, 2 * 512], F32))
    pvt = es.enter_context(nc.psum_tensor('p1_pvt', [128, 2 * 512], F32))

    x_r = x_h.rearrange('(c t p) d -> c t p d', t=4, p=128)
    t_, y_, g_ = nc.tensor, nc.sync, nc.gpsimd
    strip_done = {}
    strip_ld = {}
    qst_tok = {}
    tok_xr = {}
    tok_xev = {}
    tok_xtb = {}
    tok_vcp = {}
    tok_vev = {}

    with es:
        for tc in range(NC4):
            if tc > 0:
                wait_tok(y_, (SC.c, tok_xr[4 * tc - 1]))
            xtok = streams['xld'].issue(
                y_, [(xbuf[:, t * 1024:(t + 1) * 1024], x_r[tc, t]) for t in range(4)])
            if tc == 0:
                for g in range(2):
                    strip_ld[g] = streams['w'].issue(
                        y_, [(wstrip[:, (g % 2) * 1024:(g % 2) * 1024 + 1024],
                              wqkv_h[g % 24].rearrange('p dp j -> p (dp j)'))])

            for t in range(4):
                idx = 4 * tc + t
                xtile = xbuf[:, t * 1024:(t + 1) * 1024]
                V.wait_tok(xtok)
                SC.wait_tok(xtok)
                _emit_stats(ctx, xtile, sqscr, st, rms=True)
                SC.wait_ge(V.c.sem, V.c.n)
                if idx >= 2:
                    SC.wait_ge(sems['sTP'].sem, 8 * (idx - 1))
                tok_xr[idx] = SC.op(
                    'activation', xr[:, (idx % 2) * 1024:(idx % 2) * 1024 + 1024],
                    xtile, AF.Identity, bias=st[:, 11:12], scale=st[:, 10:11])
                t_.wait_ge(SC.c.sem, tok_xr[idx])
                if t == 0 and tc > 0:
                    t_.wait_ge(sems['sMM'].sem, 24 * tc)
                if t == 0 and tc > 0:
                    V.wait_ge(sems['sMM'].sem, 24 * tc)
                for dp in range(8):
                    k = 8 * idx + dp
                    if k >= 2:
                        t_.wait_ge(V.c.sem, tok_xev[k - 2])
                    off = (k % 2) * 512 + ((k // 2) % 4) * 128
                    sems['sTP'].inc(
                        t_.transpose(ptp[:, off:off + 128],
                                     xr[:, (idx % 2) * 1024 + dp * 128:
                                        (idx % 2) * 1024 + dp * 128 + 128],
                                     ident[:]))
                    V.wait_ge(sems['sTP'].sem, sems['sTP'].n)
                    tok_xev[k] = V.op(
                        'tensor_scalar',
                        xtb[:, dp * 512 + t * 128: dp * 512 + t * 128 + 128],
                        ptp[:, off:off + 128],
                        gam[:, dp:dp + 1], None, op0=ALU.mult)
            tok_xtb[tc] = V.c.n

            for fp in range(24):
                g = 24 * tc + fp
                par = fp % 2
                wait_tok(t_, strip_ld[g])
                if fp == 0:
                    t_.wait_ge(V.c.sem, tok_xtb[tc])
                if g >= 2:
                    wait_tok(t_, strip_done[g - 2])
                for dp in range(8):
                    mm = t_.matmul(pqkv[:, par * 512:par * 512 + 512],
                                   wstrip[:, par * 1024 + dp * 128:
                                          par * 1024 + dp * 128 + 128],
                                   xtb[:, dp * 512:(dp + 1) * 512],
                                   start=(dp == 0), stop=(dp == 7))
                sems['sMM'].inc(mm)

                if fp < 8:
                    qg = 8 * tc + fp
                    SC.wait_ge(sems['sMM'].sem, sems['sMM'].n)
                    if qg >= 2:
                        SC.wait_tok(qst_tok[qg - 2])
                    tq = SC.op('activation',
                               qstage[:, (qg % 2) * 512:(qg % 2) * 512 + 512],
                               pqkv[:, par * 512:par * 512 + 512],
                               AF.Identity, scale=cst[:, C_QSCALE:C_QSCALE + 1],
                               bias=cst[:, C_ZERO:C_ZERO + 1])
                    strip_done[g] = (SC.c, tq)
                    g_.wait_ge(SC.c.sem, tq)
                    qst_tok[qg] = streams['qst'].issue(
                        g_, [(qT_s[2 * tc + half, fp],
                              qstage[:, (qg % 2) * 512 + half * 256:
                                     (qg % 2) * 512 + half * 256 + 256])
                             for half in range(2)])
                elif fp < 16:
                    pr = fp - 8
                    V.wait_ge(sems['sMM'].sem, sems['sMM'].n)
                    tk = V.op('tensor_copy',
                              kbuf[:, pr * 2048 + tc * 512: pr * 2048 + tc * 512 + 512],
                              pqkv[:, par * 512:par * 512 + 512])
                    strip_done[g] = (V.c, tk)
                else:
                    vp = fp - 16
                    V.wait_ge(sems['sMM'].sem, sems['sMM'].n)
                    if vp == 0 and tc > 0:
                        V.wait_ge(sems['sTV'].sem, 32 * tc)
                    tv = V.op('tensor_copy', vtbuf[:, vp * 512:(vp + 1) * 512],
                              pqkv[:, par * 512:par * 512 + 512])
                    strip_done[g] = (V.c, tv)
                gn = g + 2
                if gn < 24 * NC4:
                    wait_tok(y_, strip_done[g])
                    strip_ld[gn] = streams['w'].issue(
                        y_, [(wstrip[:, (gn % 2) * 1024:(gn % 2) * 1024 + 1024],
                              wqkv_h[gn % 24].rearrange('p dp j -> p (dp j)'))])
            tok_vev[tc] = V.c.n

            t_.wait_ge(V.c.sem, tok_vev[tc])
            for vp in range(8):
                for i in range(4):
                    k = (8 * tc + vp) * 4 + i
                    jt = 4 * tc + i
                    if k >= 2:
                        t_.wait_ge(V.c.sem, tok_vcp[k - 2])
                    off = (k % 2) * 512 + ((k // 2) % 4) * 128
                    sems['sTV'].inc(
                        t_.transpose(pvt[:, off:off + 128],
                                     vtbuf[:, vp * 512 + i * 128: vp * 512 + i * 128 + 128],
                                     ident[:]))
                    V.wait_ge(sems['sTV'].sem, sems['sTV'].n)
                    if k == 0:
                        V.wait_tok(vinit_tok)
                    for hh in range(2):
                        h = 2 * vp + hh
                        tok_vcp[k] = V.op(
                            'tensor_copy',
                            vstore[:, jt * VW * H + h * VW: jt * VW * H + h * VW + 64],
                            pvt[:, off + hh * 64: off + hh * 64 + 64])


def _phase2(ctx, x_h, wout_h, qT_s, x1_s):
    nc, sems, streams = ctx['nc'], ctx['sems'], ctx['streams']
    V, SC = ctx['V'], ctx['SC']
    kbuf, vstore, ident, cst = (ctx[k] for k in ('kbuf', 'vstore', 'ident', 'cst'))
    from contextlib import ExitStack
    es = ExitStack()
    woutb = es.enter_context(nc.sbuf_tensor('p2_wout', [128, 8 * 1024], F32))
    qbuf = es.enter_context(nc.sbuf_tensor('p2_qbuf', [128, 8 * 256], F32))
    xres = es.enter_context(nc.sbuf_tensor('p2_xres', [128, 2 * 1024], F32))
    ptb = es.enter_context(nc.sbuf_tensor('p2_ptb', [128, 4 * 256], F32))
    apv = es.enter_context(nc.sbuf_tensor('p2_apv', [128, 2 * 256], F32))
    atm = es.enter_context(nc.sbuf_tensor('p2_atm', [128, 2 * 1024], F32))
    rcb = es.enter_context(nc.sbuf_tensor('p2_rcb', [128, 4], F32))
    psS = [es.enter_context(nc.psum_tensor(f'p2_psS{i}', [128, 256], F32)) for i in range(4)]
    psPV = [es.enter_context(nc.psum_tensor(f'p2_psPV{i}', [128, 256], F32)) for i in range(2)]
    psY = [es.enter_context(nc.psum_tensor(f'p2_psY{i}', [128, 512], F32)) for i in range(2)]
    psT = [psY[0], psY[1]]

    t_, y_, g_ = nc.tensor, nc.sync, nc.gpsimd
    x_r = x_h.rearrange('(c t p) d -> c t p d', t=2, p=128)
    x1_r = x1_s.rearrange('(c t p) d -> c t p d', t=2, p=128)

    tok_exp = {}
    tok_apv = {}
    tok_rc = {}
    tok_ae = {}
    tok_fe = {}
    tok_ye = {}
    x1_toks = []

    with es:
        wo_tok = streams['wo'].issue(
            y_, [(woutb[:, dp * 1024:(dp + 1) * 1024], wout_h[dp * 128:(dp + 1) * 128, :])
                 for dp in range(8)])
        for c in streams['qst'].sems:
            y_.wait_ge(c.sem, c.n)

        for qc in range(QC):
            if qc > 0:
                y_.wait_ge(sems['sWM'].sem, 4 * qc)
            q_tok = streams['ql'].issue(
                y_, [(qbuf[:, f * 256:(f + 1) * 256], qT_s[qc, f]) for f in range(8)])
            if qc > 0:
                wait_tok(y_, (V.c, tok_ye[4 * qc - 1]))
            x_tok = streams['xr2'].issue(
                y_, [(xres[:, t * 1024:(t + 1) * 1024], x_r[qc, t]) for t in range(2)])

            LAG = 3

            def emit_pv(kk):
                hj, jj = divmod(kk, 16)
                hh_ = hj % H
                t_.wait_ge(SC.c.sem, tok_exp[kk])
                if jj == 0 and hj >= 2:
                    wait_tok(t_, (V.c, tok_apv[hj - 2]))
                if jj == 0 and hh_ == 0 and qc > 0:
                    wait_tok(t_, (V.c, tok_ye[4 * qc - 1]))
                sems['sPV'].inc(
                    t_.matmul(psPV[hj % 2][0:VW, :],
                              vstore[:, jj * VW * H + hh_ * VW:
                                     jj * VW * H + hh_ * VW + VW],
                              ptb[:, (kk % 4) * 256:(kk % 4) * 256 + 256],
                              start=(jj == 0), stop=(jj == 15)))

            def emit_epilogue(e):
                he = e % H
                V.wait_ge(sems['sPV'].sem, 16 * (e + 1))
                if e >= 2:
                    V.wait_ge(sems['sTT'].sem, 2 * (e - 1))
                tok_apv[e] = V.op(
                    'tensor_copy', apv[0:VW, (e % 2) * 256:(e % 2) * 256 + 256],
                    psPV[e % 2][0:VW, :])
                t_.wait_ge(V.c.sem, tok_apv[e])
                if e >= 2:
                    t_.wait_ge(SC.c.sem, tok_ae[e - 2])
                if he <= 1 and qc > 0:
                    wait_tok(t_, (V.c, tok_ye[4 * qc - 1]))
                for tp in range(2):
                    sems['sTT'].inc(
                        t_.transpose(psT[e % 2][:, tp * 66:tp * 66 + 66],
                                     apv[0:VW, (e % 2) * 256 + tp * 128:
                                         (e % 2) * 256 + tp * 128 + 128],
                                     ident[0:VW, 0:VW]))
                V.wait_ge(sems['sTT'].sem, 2 * (e + 1))
                for tp in range(2):
                    tok_rc[e] = V.op(
                        'reciprocal', rcb[:, (e % 2) * 2 + tp:(e % 2) * 2 + tp + 1],
                        psT[e % 2][:, tp * 66 + 64: tp * 66 + 65])
                SC.wait_ge(V.c.sem, tok_rc[e])
                if he == 0 and qc > 0:
                    wait_tok(SC, x1_toks[qc - 1])
                for tp in range(2):
                    tok_ae[e] = SC.op(
                        'activation', atm[:, tp * 1024 + he * 64: tp * 1024 + he * 64 + 64],
                        psT[e % 2][:, tp * 66: tp * 66 + 64],
                        AF.Identity,
                        scale=rcb[:, (e % 2) * 2 + tp:(e % 2) * 2 + tp + 1],
                        bias=cst[:, C_ZERO:C_ZERO + 1])

            for h in range(H):
                hi = qc * H + h
                pr, hh = h // 2, h % 2
                base = hh * 64
                qslice = qbuf[base:base + 64, pr * 256:(pr + 1) * 256]
                if h == 0:
                    wait_tok(t_, q_tok)
                for j in range(16):
                    k = hi * 16 + j
                    if k >= 4:
                        t_.wait_ge(SC.c.sem, tok_exp[k - 4])
                    sems['sS'].inc(
                        t_.matmul(psS[k % 4][:, :],
                                  kbuf[base:base + 64,
                                       pr * 2048 + j * 128: pr * 2048 + j * 128 + 128],
                                  qslice, start=True, stop=True))
                    SC.wait_ge(sems['sS'].sem, k + 1)
                    if k >= 4:
                        SC.wait_ge(sems['sPV'].sem, k - 3)
                    tok_exp[k] = SC.op(
                        'activation', ptb[:, (k % 4) * 256:(k % 4) * 256 + 256],
                        psS[k % 4][:, :], AF.Exp, bias=cst[:, C_NSHIFT:C_NSHIFT + 1])
                    if k - LAG >= qc * 256:
                        emit_pv(k - LAG)
                if h >= 1:
                    emit_epilogue(hi - 1)
                if h == H - 1:
                    for kk in range((qc + 1) * 256 - LAG, (qc + 1) * 256):
                        emit_pv(kk)
                    emit_epilogue(hi)

            t_.wait_ge(SC.c.sem, tok_ae[qc * H + H - 1])
            for tp in range(2):
                for dp in range(8):
                    k = qc * 16 + tp * 8 + dp
                    if k >= 2:
                        wait_tok(t_, (V.c, tok_fe[k - 2]))
                    bank, half = k % 2, (k // 2) % 2
                    sems['sFT'].inc(
                        t_.transpose(psS[bank][:, half * 128:half * 128 + 128],
                                     atm[:, tp * 1024 + dp * 128: tp * 1024 + dp * 128 + 128],
                                     ident[:]))
                    V.wait_ge(sems['sFT'].sem, sems['sFT'].n)
                    tok_fe[k] = V.op(
                        'tensor_copy', qbuf[:, dp * 256 + tp * 128: dp * 256 + tp * 128 + 128],
                        psS[bank][:, half * 128:half * 128 + 128])

            wait_tok(t_, (V.c, tok_fe[qc * 16 + 15]))
            if qc == 0:
                wait_tok(t_, wo_tok)
            for tp in range(2):
                for nh in range(2):
                    u = 4 * qc + 2 * tp + nh
                    if u >= 2:
                        wait_tok(t_, (V.c, tok_ye[u - 2]))
                    for dp in range(8):
                        mm = t_.matmul(psY[nh][:, :],
                                       qbuf[:, dp * 256 + tp * 128: dp * 256 + tp * 128 + 128],
                                       woutb[:, dp * 1024 + nh * 512: dp * 1024 + nh * 512 + 512],
                                       start=(dp == 0), stop=(dp == 7))
                    sems['sWM'].inc(mm)
                    V.wait_ge(sems['sWM'].sem, sems['sWM'].n)
                    if tp == 0 and nh == 0:
                        V.wait_tok(x_tok)
                    tok_ye[u] = V.op(
                        'tensor_tensor',
                        atm[:, tp * 1024 + nh * 512: tp * 1024 + nh * 512 + 512],
                        xres[:, tp * 1024 + nh * 512: tp * 1024 + nh * 512 + 512],
                        psY[nh][:, :], op=ALU.add)
            g_.wait_ge(V.c.sem, tok_ye[4 * qc + 3])
            x1_toks.append(streams['x1st'].issue(
                g_, [(x1_r[qc, t], atm[:, t * 1024:(t + 1) * 1024]) for t in range(2)]))


def _phase3(ctx, x1_s, w1_h, b1_h, w2_h, b2_h, rw_h, out_h, stages=(), dbg=None):
    nc, sems, streams = ctx['nc'], ctx['sems'], ctx['streams']
    V, SC = ctx['V'], ctx['SC']
    ident, cst = ctx['ident'], ctx['cst']
    from contextlib import ExitStack
    es = ExitStack()
    ln2T = es.enter_context(nc.sbuf_tensor('p3_ln2T', [128, 8 * 2048], F32))
    yacc = es.enter_context(nc.sbuf_tensor('p3_yacc', [128, 16 * 1024], F32))
    x1t = es.enter_context(nc.sbuf_tensor('p3_x1t', [128, 2 * 1024], F32))
    ln2 = es.enter_context(nc.sbuf_tensor('p3_ln2', [128, 2 * 1024], F32))
    sqscr = es.enter_context(nc.sbuf_tensor('p3_sqscr', [128, 1024], F32))
    st = es.enter_context(nc.sbuf_tensor('p3_st', [128, 16], F32))
    rwb = es.enter_context(nc.sbuf_tensor('p3_rw', [128, 64], F32))
    rtb = es.enter_context(nc.sbuf_tensor('p3_rt', [128, 8], F32))
    rpb = es.enter_context(nc.sbuf_tensor('p3_rp', [128, 24], F32))
    fbuf = es.enter_context(nc.sbuf_tensor('p3_fb', [128, 16], F32))
    b1c = es.enter_context(nc.sbuf_tensor('p3_b1c', [128, 32], F32))
    b2b = es.enter_context(nc.sbuf_tensor('p3_b2b', [128, 1024], F32))
    w1s = es.enter_context(nc.sbuf_tensor('p3_w1s', [128, 2 * 1024], F32))
    w2s = es.enter_context(nc.sbuf_tensor('p3_w2s', [128, 2 * 1024], F32))
    hTb = es.enter_context(nc.sbuf_tensor('p3_hT', [128, 2 * 2048], F32))
    tmpb = es.enter_context(nc.sbuf_tensor('p3_tmp', [128, 1024], F32))
    rlog = es.enter_context(nc.sbuf_tensor('p3_rlog', [128, 128], F32))
    outb = es.enter_context(nc.sbuf_tensor('p3_out', [128, 2 * 1024], F32))

    t_, y_, g_ = nc.tensor, nc.sync, nc.gpsimd
    x1_r = x1_s.rearrange('(t p) d -> t p d', p=128)
    out_r = out_h.rearrange('(t p) d -> t p d', p=128)
    tp_base = sems['sTP'].n

    tok_ln = {}
    tok_le = {}
    tok_rexp = {}
    tok_fc = {}
    tok_ge = {}
    tok_ya = {}
    ost_toks = {}

    with es:
        small_tok = streams['const'].issue(
            y_, [(rwb[:, dp * 8:(dp + 1) * 8], rw_h[dp * 128:(dp + 1) * 128, :])
                 for dp in range(8)] + [(b1c[:], b1_h), (b2b[:], b2_h)])
        for c in streams['x1st'].sems:
            y_.wait_ge(c.sem, c.n)

        esA = ExitStack()
        ptr = esA.enter_context(nc.psum_tensor('p3_ptr', [128, 2 * 512], F32))
        plog = esA.enter_context(nc.psum_tensor('p3_plog', [128, 8], F32))
        with esA:
            for t in range(NT):
                if t >= 2:
                    wait_tok(y_, (SC.c, tok_ln[t - 2]))
                a_tok = streams['a'].issue(
                    y_, [(x1t[:, (t % 2) * 1024:(t % 2) * 1024 + 1024], x1_r[t])])
                xtile = x1t[:, (t % 2) * 1024:(t % 2) * 1024 + 1024]
                V.wait_tok(a_tok)
                SC.wait_tok(a_tok)
                _emit_stats(ctx, xtile, sqscr, st, rms=False)
                SC.wait_ge(V.c.sem, V.c.n)
                if t >= 2:
                    SC.wait_ge(sems['sTP'].sem, tp_base + 8 * (t - 1))
                tok_ln[t] = SC.op(
                    'activation', ln2[:, (t % 2) * 1024:(t % 2) * 1024 + 1024], xtile,
                    AF.Identity, bias=st[:, 11:12], scale=st[:, 10:11])
                t_.wait_ge(SC.c.sem, tok_ln[t])
                for dp in range(8):
                    k = 8 * t + dp
                    if k >= 2:
                        wait_tok(t_, (V.c, tok_le[k - 2]))
                    off = (k % 2) * 512 + ((k // 2) % 4) * 128
                    sems['sTP'].inc(
                        t_.transpose(ptr[:, off:off + 128],
                                     ln2[:, (t % 2) * 1024 + dp * 128:
                                         (t % 2) * 1024 + dp * 128 + 128],
                                     ident[:]))
                    V.wait_ge(sems['sTP'].sem, sems['sTP'].n)
                    tok_le[k] = V.op(
                        'tensor_copy', ln2T[:, dp * 2048 + t * 128: dp * 2048 + t * 128 + 128],
                        ptr[:, off:off + 128])
                wait_tok(t_, (V.c, tok_le[8 * t + 7]))
                if t == 0:
                    wait_tok(t_, small_tok)
                if t >= 1:
                    wait_tok(t_, (V.c, tok_rexp[t - 1]))
                for dp in range(8):
                    mm = t_.matmul(plog[:, :],
                                   ln2T[:, dp * 2048 + t * 128: dp * 2048 + t * 128 + 128],
                                   rwb[:, dp * 8:(dp + 1) * 8],
                                   start=(dp == 0), stop=(dp == 7))
                sems['sRT'].inc(mm)
                V.wait_ge(sems['sRT'].sem, sems['sRT'].n)
                tok_rexp[t] = V.op('tensor_copy', rlog[:, t * 8:(t + 1) * 8], plog[:, :])
            for t in range(NT):
                V.op('reduce_max', rtb[:, 0:1], rlog[:, t * 8:(t + 1) * 8], axis=AX.X)
                t_nm = V.op('tensor_scalar', rtb[:, 1:2], rtb[:, 0:1], -1.0, None,
                            op0=ALU.mult)
                SC.wait_ge(V.c.sem, t_nm)
                t_re = SC.op('activation', rpb[:, 0:8], rlog[:, t * 8:(t + 1) * 8],
                             AF.Exp, bias=rtb[:, 1:2])
                V.wait_ge(SC.c.sem, t_re)
                V.op('reduce_sum', rtb[:, 2:3], rpb[:, 0:8], axis=AX.X)
                V.op('reciprocal', rtb[:, 3:4], rtb[:, 2:3])
                V.op('tensor_scalar', rpb[:, 8:16], rpb[:, 0:8], rtb[:, 3:4], None,
                     op0=ALU.mult)
                V.op('reduce_max', rtb[:, 4:5], rpb[:, 8:16], axis=AX.X)
                V.op('tensor_scalar', rpb[:, 16:24], rpb[:, 8:16], rtb[:, 4:5], None,
                     op0=ALU.is_equal)
                V.op('scalar_tensor_tensor', rpb[:, 16:24], rpb[:, 16:24], -1e30,
                     rpb[:, 8:16], op0=ALU.mult, op1=ALU.add)
                V.op('reduce_max', rtb[:, 5:6], rpb[:, 16:24], axis=AX.X)
                V.op('tensor_tensor', rtb[:, 6:7], rtb[:, 4:5], rtb[:, 5:6], op=ALU.add)
                V.op('reciprocal', rtb[:, 7:8], rtb[:, 6:7])
                tok_fc[t] = V.op('tensor_tensor', fbuf[:, t:t + 1], rtb[:, 6:7],
                                 rtb[:, 7:8], op=ALU.mult)
        hsS_afterA = SC.c.n
        tok_lnT_all = tok_le[8 * NT - 8 + 7]
        if 'p3a' in stages:
            g_.wait_ge(V.c.sem, V.c.n)
            streams['dbg'].issue(g_, [(dbg['ln2T'], ln2T[:]), (dbg['fb'], fbuf[:])])
            return

        ph = [es.enter_context(nc.psum_tensor(f'p3_ph{i}', [128, 512], F32)) for i in range(2)]
        py = [es.enter_context(nc.psum_tensor(f'p3_py{i}', [128, 1024], F32)) for i in range(2)]
        w1_toks, w2_toks = {}, {}
        for fg in range(16):
            for fp2 in range(2):
                fp = 2 * fg + fp2
                if fp >= 2:
                    y_.wait_ge(sems['sHM'].sem, 4 * (fp - 1))
                w1_toks[fp] = streams['w1'].issue(
                    y_, [(w1s[:, (fp % 2) * 1024:(fp % 2) * 1024 + 1024],
                          w1_h[fp].rearrange('p dp j -> p (dp j)'))])
                if fg >= 1:
                    y_.wait_ge(sems['sWM2'].sem, 16 * fg)
                w2_toks[fp] = streams['w2'].issue(
                    y_, [(w2s[:, fp2 * 1024:fp2 * 1024 + 1024],
                          w2_h[fp * 128:(fp + 1) * 128, :])])
                for tc4 in range(4):
                    m = 4 * fp + tc4
                    wait_tok(t_, w1_toks[fp])
                    if fp == 0 and tc4 == 0:
                        wait_tok(t_, (V.c, tok_lnT_all))
                        t_.wait_ge(SC.c.sem, hsS_afterA)
                    if m >= 2:
                        wait_tok(t_, (SC.c, tok_ge[m - 2]))
                    for dp in range(8):
                        mm = t_.matmul(ph[m % 2][:, :],
                                       w1s[:, (fp % 2) * 1024 + dp * 128:
                                           (fp % 2) * 1024 + dp * 128 + 128],
                                       ln2T[:, dp * 2048 + tc4 * 512:
                                            dp * 2048 + tc4 * 512 + 512],
                                       start=(dp == 0), stop=(dp == 7))
                    sems['sHM'].inc(mm)
                    SC.wait_ge(sems['sHM'].sem, sems['sHM'].n)
                    if fg >= 1 and tc4 == 0 and fp2 == 0:
                        SC.wait_ge(sems['sWM2'].sem, 16 * fg)
                    tok_ge[m] = SC.op(
                        'activation', hTb[:, fp2 * 2048 + tc4 * 512:
                                          fp2 * 2048 + tc4 * 512 + 512],
                        ph[m % 2][:, :], AF.Gelu_apprx_tanh, bias=b1c[:, fp:fp + 1])
            for t16 in range(NT):
                w = 16 * fg + t16
                t_.wait_ge(SC.c.sem, tok_ge[4 * (2 * fg + 1) + 3])
                wait_tok(t_, w2_toks[2 * fg])
                wait_tok(t_, w2_toks[2 * fg + 1])
                if w >= 2:
                    wait_tok(t_, (V.c, tok_ya[w - 2]))
                for nh in range(2):
                    for fp2 in range(2):
                        mm = t_.matmul(py[w % 2][:, nh * 512:nh * 512 + 512],
                                       hTb[:, fp2 * 2048 + t16 * 128:
                                           fp2 * 2048 + t16 * 128 + 128],
                                       w2s[:, fp2 * 1024 + nh * 512:
                                           fp2 * 1024 + nh * 512 + 512],
                                       start=(fp2 == 0), stop=(fp2 == 1))
                sems['sWM2'].inc(mm)
                V.wait_ge(sems['sWM2'].sem, sems['sWM2'].n)
                if fg == 0:
                    tok_ya[w] = V.op('tensor_copy', yacc[:, t16 * 1024:(t16 + 1) * 1024],
                                     py[w % 2][:, :])
                else:
                    tok_ya[w] = V.op('tensor_tensor', yacc[:, t16 * 1024:(t16 + 1) * 1024],
                                     yacc[:, t16 * 1024:(t16 + 1) * 1024],
                                     py[w % 2][:, :], op=ALU.add)

        if 'p3b' in stages:
            g_.wait_ge(V.c.sem, V.c.n)
            streams['dbg'].issue(g_, [(dbg['yacc'], yacc[:])])
            return

        tok_oadd = {}
        tok_osc = {}
        for t in range(NT):
            if t >= 2:
                wait_tok(y_, (V.c, tok_oadd[t - 2]))
            c_tok = streams['a'].issue(
                y_, [(x1t[:, (t % 2) * 1024:(t % 2) * 1024 + 1024], x1_r[t])])
            if t >= 1:
                V.wait_ge(SC.c.sem, tok_osc[t - 1])
            t_tmp = V.op('tensor_tensor', tmpb[:], yacc[:, t * 1024:(t + 1) * 1024],
                         b2b[:], op=ALU.add)
            SC.wait_ge(V.c.sem, t_tmp)
            SC.wait_ge(V.c.sem, tok_fc[NT - 1])
            if t >= 2:
                wait_tok(SC, ost_toks[t - 2])
            tok_osc[t] = SC.op(
                'activation', outb[:, (t % 2) * 1024:(t % 2) * 1024 + 1024], tmpb[:],
                AF.Identity, scale=fbuf[:, t:t + 1], bias=cst[:, C_ZERO:C_ZERO + 1])
            V.wait_ge(SC.c.sem, tok_osc[t])
            V.wait_tok(c_tok)
            tok_oadd[t] = V.op(
                'tensor_tensor', outb[:, (t % 2) * 1024:(t % 2) * 1024 + 1024],
                outb[:, (t % 2) * 1024:(t % 2) * 1024 + 1024],
                x1t[:, (t % 2) * 1024:(t % 2) * 1024 + 1024], op=ALU.add)
            g_.wait_ge(V.c.sem, tok_oadd[t])
            ost_toks[t] = streams['ost'].issue(
                g_, [(out_r[t], outb[:, (t % 2) * 1024:(t % 2) * 1024 + 1024])])


_NC_CACHE = {}


def kernel(L0_attn_x, L0_attn_gamma, L0_attn_w_qkv, L0_attn_w_out,
           L0_ffn_router_w, L0_ffn_w1, L0_ffn_b1, L0_ffn_w2, L0_ffn_b2):
    from concourse.bass_utils import run_bass_kernel_spmd
    x_full = np.asarray(L0_attn_x, np.float32)
    n = 8
    base = host_inputs(x_full[0], L0_attn_gamma, L0_attn_w_qkv, L0_attn_w_out,
                       L0_ffn_router_w, L0_ffn_w1, L0_ffn_b1, L0_ffn_w2, L0_ffn_b2)
    in_maps = []
    for b in range(n):
        m = dict(base)
        m['x'] = np.ascontiguousarray(x_full[b])
        in_maps.append(m)
    if 'nc' not in _NC_CACHE:
        _NC_CACHE['nc'] = build()
    res = run_bass_kernel_spmd(_NC_CACHE['nc'], in_maps, core_ids=list(range(n)))
    out = np.stack([np.asarray(r['out'], np.float32) for r in res.results], axis=0)
    return out


# version 03
# speedup vs baseline: 1.3351x; 1.0351x over previous
import numpy as np
import concourse.bass as bass
import concourse.mybir as mybir

F32 = mybir.dt.float32
AF = mybir.ActivationFunctionType
ALU = mybir.AluOpType
AX = mybir.AxisListType

S = 2048
D = 1024
H = 16
F = 4096
NT = S // 128
NC4 = 4
QC = 8
VW = 66
SHIFT = 20.0
C_NSHIFT, C_QSCALE, C_ONE, C_ZERO = 0, 1, 2, 3


def host_inputs(x, gamma, wqkv, wout, router_w, w1, b1, w2, b2):
    d = {}
    d['x'] = np.ascontiguousarray(np.asarray(x, np.float32))
    d['gamma_c'] = np.ascontiguousarray(np.asarray(gamma, np.float32).reshape(8, 128).T)
    d['wqkv_bl'] = np.ascontiguousarray(
        np.asarray(wqkv, np.float32).reshape(8, 128, 24, 128).transpose(2, 1, 0, 3))
    d['wout'] = np.ascontiguousarray(np.asarray(wout, np.float32))
    d['rw'] = np.ascontiguousarray(np.asarray(router_w, np.float32))
    d['w1_bl'] = np.ascontiguousarray(
        np.asarray(w1, np.float32).reshape(8, 128, 32, 128).transpose(2, 1, 0, 3))
    d['b1_c'] = np.ascontiguousarray(np.asarray(b1, np.float32).reshape(32, 128).T)
    d['w2'] = np.ascontiguousarray(np.asarray(w2, np.float32))
    d['b2_bc'] = np.ascontiguousarray(np.broadcast_to(np.asarray(b2, np.float32), (128, D)))
    d['ident'] = np.eye(128, dtype=np.float32)
    cst = np.zeros((128, 4), np.float32)
    cst[:, C_NSHIFT] = -SHIFT
    cst[:, C_QSCALE] = 0.125
    cst[:, C_ONE] = 1.0
    d['cst'] = cst
    return d


class Ctr:
    def __init__(self, sem):
        self.sem = sem
        self.n = 0

    def inc(self, inst, by=1):
        inst.then_inc(self.sem, by)
        self.n += by
        return self.n

    def wait(self, eng, target=None):
        eng.wait_ge(self.sem, self.n if target is None else target)


class ChEng:

    def __init__(self, eng, ctr):
        self.e = eng
        self.c = ctr

    def op(self, name, *a, **k):
        self.e.wait_ge(self.c.sem, self.c.n)
        inst = getattr(self.e, name)(*a, **k)
        self.c.inc(inst)
        return self.c.n

    def wait_ge(self, sem, n):
        self.e.wait_ge(sem, n)

    def wait_tok(self, tok):
        c, n = tok
        self.e.wait_ge(c.sem, n)

    @property
    def tok(self):
        return (self.c, self.c.n)


class DmaStream:

    def __init__(self, nc, es, name, nbuf=1):
        self.sems = [Ctr(es.enter_context(nc.semaphore(f'{name}{i}'))) for i in range(nbuf)]
        self.nbuf = nbuf
        self.i = 0

    def issue(self, eng, pairs):
        c = self.sems[self.i % self.nbuf]
        eng.wait_ge(c.sem, c.n)
        for out, in_ in pairs:
            c.inc(eng.dma_start(out, in_), 16)
        self.i += 1
        return (c, c.n)


def wait_tok(eng, tok):
    c, n = tok
    eng.wait_ge(c.sem, n)


def build():
    phases = (1, 2, 3)
    debug = ()
    nc = bass.Bass('TRN2', target_bir_lowering=False)

    x_h = nc.dram_tensor('x', [S, D], F32, kind='ExternalInput').ap()
    gamma_h = nc.dram_tensor('gamma_c', [128, 8], F32, kind='ExternalInput').ap()
    wqkv_h = nc.dram_tensor('wqkv_bl', [24, 128, 8, 128], F32, kind='ExternalInput').ap()
    wout_h = nc.dram_tensor('wout', [D, D], F32, kind='ExternalInput').ap()
    rw_h = nc.dram_tensor('rw', [D, 8], F32, kind='ExternalInput').ap()
    w1_h = nc.dram_tensor('w1_bl', [32, 128, 8, 128], F32, kind='ExternalInput').ap()
    b1_h = nc.dram_tensor('b1_c', [128, 32], F32, kind='ExternalInput').ap()
    w2_h = nc.dram_tensor('w2', [F, D], F32, kind='ExternalInput').ap()
    b2_h = nc.dram_tensor('b2_bc', [128, D], F32, kind='ExternalInput').ap()
    ident_h = nc.dram_tensor('ident', [128, 128], F32, kind='ExternalInput').ap()
    cst_h = nc.dram_tensor('cst', [128, 4], F32, kind='ExternalInput').ap()

    qT_s = nc.dram_tensor('qT_s', [QC, 8, 128, 256], F32, kind='Internal').ap()
    x1_s = nc.dram_tensor('x1_s', [S, D], F32,
                          kind='ExternalInput' if 'x1in' in debug else 'Internal').ap()
    out_h = nc.dram_tensor('out', [S, D], F32, kind='ExternalOutput').ap()
    dbg = {}
    if 'p1' in debug:
        dbg['kT'] = nc.dram_tensor('dbg_kT', [128, 8 * 2048], F32, kind='ExternalOutput').ap()
        dbg['vst'] = nc.dram_tensor('dbg_vst', [128, NT * VW * H], F32, kind='ExternalOutput').ap()
        dbg['qT'] = nc.dram_tensor('dbg_qT', [QC, 8, 128, 256], F32, kind='ExternalOutput').ap()
    if 'p2' in debug:
        dbg['x1'] = nc.dram_tensor('dbg_x1', [S, D], F32, kind='ExternalOutput').ap()
    if 'p3a' in debug:
        dbg['ln2T'] = nc.dram_tensor('dbg_ln2T', [128, 8 * 2048], F32, kind='ExternalOutput').ap()
        dbg['fb'] = nc.dram_tensor('dbg_fb', [128, 16], F32, kind='ExternalOutput').ap()
    if 'p3b' in debug:
        dbg['yacc'] = nc.dram_tensor('dbg_yacc', [128, 16 * 1024], F32, kind='ExternalOutput').ap()

    from contextlib import ExitStack
    es = ExitStack()
    ident = es.enter_context(nc.sbuf_tensor('ident_sb', [128, 128], F32))
    cst = es.enter_context(nc.sbuf_tensor('cst_sb', [128, 4], F32))
    gam = es.enter_context(nc.sbuf_tensor('gam_sb', [128, 8], F32))
    gam1 = es.enter_context(nc.sbuf_tensor('gam1_sb', [128, 8], F32))
    es12 = ExitStack()
    kbuf = es12.enter_context(nc.sbuf_tensor('kbuf', [128, 8 * 2048], F32))
    vstore = es12.enter_context(nc.sbuf_tensor('vstore', [128, NT * VW * H], F32))

    sem_names = ['cV', 'cS', 'sTP', 'sTV', 'sMM', 'sS', 'sPV', 'sTT', 'sFT',
                 'sRT', 'sWM', 'sHM', 'sWM2', 'sBAR']
    sems = {name: Ctr(es.enter_context(nc.semaphore(name))) for name in sem_names}

    streams = {
        'const': DmaStream(nc, es, 'dCON', 1),
        'xld': DmaStream(nc, es, 'dX', 1),
        'w': DmaStream(nc, es, 'dW', 2),
        'qst': DmaStream(nc, es, 'dQS', 2),
        'ql': DmaStream(nc, es, 'dQL', 1),
        'xr2': DmaStream(nc, es, 'dXR', 1),
        'wo': DmaStream(nc, es, 'dWO', 1),
        'x1st': DmaStream(nc, es, 'dX1', 1),
        'a': DmaStream(nc, es, 'dA', 2),
        'w1': DmaStream(nc, es, 'dW1', 2),
        'w2': DmaStream(nc, es, 'dW2', 2),
        'ost': DmaStream(nc, es, 'dOS', 2),
        'dbg': DmaStream(nc, es, 'dDB', 1),
    }

    V = ChEng(nc.vector, sems['cV'])
    SC = ChEng(nc.scalar, sems['cS'])
    ctx = dict(nc=nc, sems=sems, streams=streams, V=V, SC=SC,
               kbuf=kbuf, vstore=vstore, ident=ident, cst=cst, gam=gam1)

    with es:
        tok_const = streams['const'].issue(
            nc.sync, [(ident[:], ident_h), (cst[:], cst_h), (gam[:], gamma_h)])
        vst_blk = vstore[:].rearrange('p (b w) -> p b w', w=VW)
        V.wait_tok(tok_const)
        V.op('memset', vst_blk[:, :, 64:65], 1.0)
        V.op('memset', vst_blk[:, :, 65:66], 0.0)
        vinit_tok = V.tok
        V.op('tensor_scalar', gam1[:], gam[:], 1.0, None, op0=ALU.add)
        for eng in (nc.tensor, nc.scalar, nc.sync, nc.gpsimd):
            wait_tok(eng, tok_const)
        nbar = 0

        if 1 in phases:
            _phase1(ctx, x_h, wqkv_h, qT_s, vinit_tok)
            nbar += 1
            _barrier(nc, sems['sBAR'], nbar, chains=(V, SC))
        if 2 in phases:
            _phase2(ctx, x_h, wout_h, qT_s, x1_s)
            nbar += 1
            _barrier(nc, sems['sBAR'], nbar, chains=(V, SC))
        if 3 in phases:
            es12.close()
            _phase3(ctx, x1_s, w1_h, b1_h, w2_h, b2_h, rw_h, out_h,
                    stages=[d for d in debug if d in ('p3a', 'p3b')], dbg=dbg)
            nbar += 1
            _barrier(nc, sems['sBAR'], nbar, chains=(V, SC))

        if 3 not in phases:
            es12.close()
        pairs = []
        if 'p1' in debug:
            pairs += [(dbg['kT'], kbuf[:]), (dbg['vst'], vstore[:]), (dbg['qT'], qT_s)]
        if 'p2' in debug:
            pairs += [(dbg['x1'], x1_s)]
        if pairs:
            for stn in ('qst', 'x1st'):
                for c in streams[stn].sems:
                    nc.gpsimd.wait_ge(c.sem, c.n)
            streams['dbg'].issue(nc.gpsimd, pairs)
        for stn in ('ost', 'dbg'):
            for c in streams[stn].sems:
                nc.gpsimd.wait_ge(c.sem, c.n)
    return nc


def _barrier(nc, bar, k, chains=()):
    for c in chains:
        c.e.wait_ge(c.c.sem, c.c.n)
    for eng in (nc.tensor, nc.vector, nc.scalar, nc.sync, nc.gpsimd):
        bar.inc(eng.nop())
    for eng in (nc.tensor, nc.vector, nc.scalar, nc.sync, nc.gpsimd):
        eng.wait_ge(bar.sem, 5 * k)


def _emit_stats(ctx, xtile, sqscr, st, rms):
    V, SC = ctx['V'], ctx['SC']
    t_sq = SC.op('activation', sqscr[:], xtile, AF.Square, accum_out=st[:, 1:2])
    V.op('reduce_sum', st[:, 0:1], xtile, axis=AX.X)
    V.wait_ge(SC.c.sem, t_sq)
    V.op('tensor_scalar', st[:, 2:3], st[:, 0:1], 1.0 / D, None, op0=ALU.mult)
    V.op('tensor_tensor', st[:, 3:4], st[:, 2:3], st[:, 2:3], op=ALU.mult)
    V.op('tensor_scalar', st[:, 4:5], st[:, 1:2], 1.0 / D, None, op0=ALU.mult)
    V.op('tensor_tensor', st[:, 5:6], st[:, 4:5], st[:, 3:4], op=ALU.subtract)
    V.op('tensor_scalar', st[:, 6:7], st[:, 5:6], 1e-5, None, op0=ALU.add)
    t_rv = V.op('reciprocal', st[:, 7:8], st[:, 6:7])
    if rms:
        V.op('tensor_scalar', st[:, 8:9], st[:, 5:6], float(D), None, op0=ALU.mult)
        t_ss2 = V.op('tensor_tensor', st[:, 8:9], st[:, 8:9], st[:, 7:8], op=ALU.mult)
        SC.wait_ge(V.c.sem, t_ss2)
        t_sqr = SC.op('activation', st[:, 12:14], st[:, 7:9], AF.Sqrt)
        V.wait_ge(SC.c.sem, t_sqr)
        V.op('tensor_scalar', st[:, 9:10], st[:, 13:14], 1e-12, None, op0=ALU.max)
        V.op('reciprocal', st[:, 9:10], st[:, 9:10])
        V.op('tensor_scalar', st[:, 10:11], st[:, 12:13], st[:, 9:10], 32.0,
             op0=ALU.mult, op1=ALU.mult)
    else:
        SC.wait_ge(V.c.sem, t_rv)
        t_sqr = SC.op('activation', st[:, 10:11], st[:, 7:8], AF.Sqrt)
        V.wait_ge(SC.c.sem, t_sqr)
    t_b = V.op('tensor_scalar', st[:, 11:12], st[:, 2:3], st[:, 10:11], -1.0,
               op0=ALU.mult, op1=ALU.mult)
    return t_b


def _phase1(ctx, x_h, wqkv_h, qT_s, vinit_tok):
    nc, sems, streams = ctx['nc'], ctx['sems'], ctx['streams']
    V, SC = ctx['V'], ctx['SC']
    kbuf, vstore, ident, cst, gam = (ctx[k] for k in ('kbuf', 'vstore', 'ident', 'cst', 'gam'))
    from contextlib import ExitStack
    es = ExitStack()
    xbuf = es.enter_context(nc.sbuf_tensor('p1_xbuf', [128, 4 * 1024], F32))
    xr = es.enter_context(nc.sbuf_tensor('p1_xr', [128, 2 * 1024], F32))
    sqscr = es.enter_context(nc.sbuf_tensor('p1_sqscr', [128, 1024], F32))
    st = es.enter_context(nc.sbuf_tensor('p1_st', [128, 16], F32))
    xtb = es.enter_context(nc.sbuf_tensor('p1_xtb', [128, 8 * 512], F32))
    wstrip = es.enter_context(nc.sbuf_tensor('p1_wstrip', [128, 2 * 1024], F32))
    qstage = es.enter_context(nc.sbuf_tensor('p1_qstage', [128, 2 * 512], F32))
    vtbuf = es.enter_context(nc.sbuf_tensor('p1_vtbuf', [128, 8 * 512], F32))
    ptp = es.enter_context(nc.psum_tensor('p1_ptp', [128, 2 * 512], F32))
    pqkv = es.enter_context(nc.psum_tensor('p1_pqkv', [128, 2 * 512], F32))
    pvt = es.enter_context(nc.psum_tensor('p1_pvt', [128, 2 * 512], F32))

    x_r = x_h.rearrange('(c t p) d -> c t p d', t=4, p=128)
    t_, y_, g_ = nc.tensor, nc.sync, nc.gpsimd
    strip_done = {}
    strip_ld = {}
    qst_tok = {}
    tok_xr = {}
    tok_xev = {}
    tok_xtb = {}
    tok_vcp = {}
    tok_vev = {}

    with es:
        for tc in range(NC4):
            if tc > 0:
                wait_tok(y_, (SC.c, tok_xr[4 * tc - 1]))
            xtok = streams['xld'].issue(
                y_, [(xbuf[:, t * 1024:(t + 1) * 1024], x_r[tc, t]) for t in range(4)])
            if tc == 0:
                for g in range(2):
                    strip_ld[g] = streams['w'].issue(
                        y_, [(wstrip[:, (g % 2) * 1024:(g % 2) * 1024 + 1024],
                              wqkv_h[g % 24].rearrange('p dp j -> p (dp j)'))])

            for t in range(4):
                idx = 4 * tc + t
                xtile = xbuf[:, t * 1024:(t + 1) * 1024]
                V.wait_tok(xtok)
                SC.wait_tok(xtok)
                _emit_stats(ctx, xtile, sqscr, st, rms=True)
                SC.wait_ge(V.c.sem, V.c.n)
                if idx >= 2:
                    SC.wait_ge(sems['sTP'].sem, 8 * (idx - 1))
                tok_xr[idx] = SC.op(
                    'activation', xr[:, (idx % 2) * 1024:(idx % 2) * 1024 + 1024],
                    xtile, AF.Identity, bias=st[:, 11:12], scale=st[:, 10:11])
                t_.wait_ge(SC.c.sem, tok_xr[idx])
                if t == 0 and tc > 0:
                    t_.wait_ge(sems['sMM'].sem, 24 * tc)
                if t == 0 and tc > 0:
                    V.wait_ge(sems['sMM'].sem, 24 * tc)
                for dp in range(8):
                    k = 8 * idx + dp
                    if k >= 2:
                        t_.wait_ge(V.c.sem, tok_xev[k - 2])
                    off = (k % 2) * 512 + ((k // 2) % 4) * 128
                    sems['sTP'].inc(
                        t_.transpose(ptp[:, off:off + 128],
                                     xr[:, (idx % 2) * 1024 + dp * 128:
                                        (idx % 2) * 1024 + dp * 128 + 128],
                                     ident[:]))
                    V.wait_ge(sems['sTP'].sem, sems['sTP'].n)
                    tok_xev[k] = V.op(
                        'tensor_scalar',
                        xtb[:, dp * 512 + t * 128: dp * 512 + t * 128 + 128],
                        ptp[:, off:off + 128],
                        gam[:, dp:dp + 1], None, op0=ALU.mult)
            tok_xtb[tc] = V.c.n

            for fp in range(24):
                g = 24 * tc + fp
                par = fp % 2
                wait_tok(t_, strip_ld[g])
                if fp == 0:
                    t_.wait_ge(V.c.sem, tok_xtb[tc])
                if g >= 2:
                    wait_tok(t_, strip_done[g - 2])
                for dp in range(8):
                    mm = t_.matmul(pqkv[:, par * 512:par * 512 + 512],
                                   wstrip[:, par * 1024 + dp * 128:
                                          par * 1024 + dp * 128 + 128],
                                   xtb[:, dp * 512:(dp + 1) * 512],
                                   start=(dp == 0), stop=(dp == 7))
                sems['sMM'].inc(mm)

                if fp < 8:
                    qg = 8 * tc + fp
                    SC.wait_ge(sems['sMM'].sem, sems['sMM'].n)
                    if qg >= 2:
                        SC.wait_tok(qst_tok[qg - 2])
                    tq = SC.op('activation',
                               qstage[:, (qg % 2) * 512:(qg % 2) * 512 + 512],
                               pqkv[:, par * 512:par * 512 + 512],
                               AF.Identity, scale=cst[:, C_QSCALE:C_QSCALE + 1],
                               bias=cst[:, C_ZERO:C_ZERO + 1])
                    strip_done[g] = (SC.c, tq)
                    g_.wait_ge(SC.c.sem, tq)
                    qst_tok[qg] = streams['qst'].issue(
                        g_, [(qT_s[2 * tc + half, fp],
                              qstage[:, (qg % 2) * 512 + half * 256:
                                     (qg % 2) * 512 + half * 256 + 256])
                             for half in range(2)])
                elif fp < 16:
                    pr = fp - 8
                    V.wait_ge(sems['sMM'].sem, sems['sMM'].n)
                    tk = V.op('tensor_copy',
                              kbuf[:, pr * 2048 + tc * 512: pr * 2048 + tc * 512 + 512],
                              pqkv[:, par * 512:par * 512 + 512])
                    strip_done[g] = (V.c, tk)
                else:
                    vp = fp - 16
                    V.wait_ge(sems['sMM'].sem, sems['sMM'].n)
                    if vp == 0 and tc > 0:
                        V.wait_ge(sems['sTV'].sem, 32 * tc)
                    tv = V.op('tensor_copy', vtbuf[:, vp * 512:(vp + 1) * 512],
                              pqkv[:, par * 512:par * 512 + 512])
                    strip_done[g] = (V.c, tv)
                gn = g + 2
                if gn < 24 * NC4:
                    wait_tok(y_, strip_done[g])
                    strip_ld[gn] = streams['w'].issue(
                        y_, [(wstrip[:, (gn % 2) * 1024:(gn % 2) * 1024 + 1024],
                              wqkv_h[gn % 24].rearrange('p dp j -> p (dp j)'))])
            tok_vev[tc] = V.c.n

            t_.wait_ge(V.c.sem, tok_vev[tc])
            for vp in range(8):
                for i in range(4):
                    k = (8 * tc + vp) * 4 + i
                    jt = 4 * tc + i
                    if k >= 2:
                        t_.wait_ge(V.c.sem, tok_vcp[k - 2])
                    off = (k % 2) * 512 + ((k // 2) % 4) * 128
                    sems['sTV'].inc(
                        t_.transpose(pvt[:, off:off + 128],
                                     vtbuf[:, vp * 512 + i * 128: vp * 512 + i * 128 + 128],
                                     ident[:]))
                    V.wait_ge(sems['sTV'].sem, sems['sTV'].n)
                    if k == 0:
                        V.wait_tok(vinit_tok)
                    for hh in range(2):
                        h = 2 * vp + hh
                        tok_vcp[k] = V.op(
                            'tensor_copy',
                            vstore[:, jt * VW * H + h * VW: jt * VW * H + h * VW + 64],
                            pvt[:, off + hh * 64: off + hh * 64 + 64])


def _phase2(ctx, x_h, wout_h, qT_s, x1_s):
    nc, sems, streams = ctx['nc'], ctx['sems'], ctx['streams']
    V, SC = ctx['V'], ctx['SC']
    kbuf, vstore, ident, cst = (ctx[k] for k in ('kbuf', 'vstore', 'ident', 'cst'))
    from contextlib import ExitStack
    es = ExitStack()
    QW = 512
    NQ = S // QW
    woutb = es.enter_context(nc.sbuf_tensor('p2_wout', [128, 2 * 1024], F32))
    qbuf = es.enter_context(nc.sbuf_tensor('p2_qbuf', [128, 8 * QW], F32))
    xres = es.enter_context(nc.sbuf_tensor('p2_xres', [128, 4 * 1024], F32))
    ptb = es.enter_context(nc.sbuf_tensor('p2_ptb', [128, 4 * QW], F32))
    apv = es.enter_context(nc.sbuf_tensor('p2_apv', [128, 2 * QW], F32))
    atm = es.enter_context(nc.sbuf_tensor('p2_atm', [128, 4 * 1024], F32))
    rcb = es.enter_context(nc.sbuf_tensor('p2_rcb', [128, 8], F32))
    psS = [es.enter_context(nc.psum_tensor(f'p2_psS{i}', [128, QW], F32)) for i in range(4)]
    psPV = [es.enter_context(nc.psum_tensor(f'p2_psPV{i}', [128, QW], F32)) for i in range(2)]
    psY = [es.enter_context(nc.psum_tensor(f'p2_psY{i}', [128, 512], F32)) for i in range(2)]
    psT = [psY[0], psY[1]]

    t_, y_, g_ = nc.tensor, nc.sync, nc.gpsimd
    x_r = x_h.rearrange('(c t p) d -> c t p d', t=4, p=128)
    x1_r = x1_s.rearrange('(c t p) d -> c t p d', t=4, p=128)

    tok_exp = {}
    tok_apv = {}
    tok_rc = {}
    tok_ae = {}
    tok_fe = {}
    tok_ye = {}
    x1_toks = []
    wos_tok = {}
    swm_tok = {}

    with es:
        for c in streams['qst'].sems:
            y_.wait_ge(c.sem, c.n)

        for qc in range(NQ):
            if qc > 0:
                y_.wait_ge(sems['sWM'].sem, 32 * qc)
            q_tok = streams['ql'].issue(
                y_, [(qbuf[:, (2 * f + half) * 256:(2 * f + half) * 256 + 256],
                      qT_s[2 * qc + half, f])
                     for f in range(8) for half in range(2)])
            if qc > 0:
                wait_tok(y_, (V.c, tok_ye[8 * qc - 1]))
            x_tok = streams['xr2'].issue(
                y_, [(xres[:, t * 1024:(t + 1) * 1024], x_r[qc, t]) for t in range(4)])

            LAG = 3

            def emit_pv(kk):
                hj, jj = divmod(kk, 16)
                hh_ = hj % H
                t_.wait_ge(SC.c.sem, tok_exp[kk])
                if jj == 0 and hj >= 2:
                    wait_tok(t_, (V.c, tok_apv[hj - 2]))
                sems['sPV'].inc(
                    t_.matmul(psPV[hj % 2][0:VW, :],
                              vstore[:, jj * VW * H + hh_ * VW:
                                     jj * VW * H + hh_ * VW + VW],
                              ptb[:, (kk % 4) * QW:(kk % 4) * QW + QW],
                              start=(jj == 0), stop=(jj == 15)))

            def emit_epilogue(e):
                he = e % H
                V.wait_ge(sems['sPV'].sem, 16 * (e + 1))
                if e >= 2:
                    V.wait_ge(sems['sTT'].sem, 4 * (e - 1))
                tok_apv[e] = V.op(
                    'tensor_copy', apv[0:VW, (e % 2) * QW:(e % 2) * QW + QW],
                    psPV[e % 2][0:VW, :])
                t_.wait_ge(V.c.sem, tok_apv[e])
                if e >= 2:
                    t_.wait_ge(SC.c.sem, tok_ae[e - 2])
                if he <= 1 and qc > 0:
                    wait_tok(t_, (V.c, tok_ye[8 * qc - 1]))
                for tp in range(4):
                    sems['sTT'].inc(
                        t_.transpose(psT[e % 2][:, tp * 66:tp * 66 + 66],
                                     apv[0:VW, (e % 2) * QW + tp * 128:
                                         (e % 2) * QW + tp * 128 + 128],
                                     ident[0:VW, 0:VW]))
                V.wait_ge(sems['sTT'].sem, 4 * (e + 1))
                for tp in range(4):
                    tok_rc[e] = V.op(
                        'reciprocal', rcb[:, (e % 2) * 4 + tp:(e % 2) * 4 + tp + 1],
                        psT[e % 2][:, tp * 66 + 64: tp * 66 + 65])
                SC.wait_ge(V.c.sem, tok_rc[e])
                if he == 0 and qc > 0:
                    wait_tok(SC, x1_toks[qc - 1])
                for tp in range(4):
                    tok_ae[e] = SC.op(
                        'activation', atm[:, tp * 1024 + he * 64: tp * 1024 + he * 64 + 64],
                        psT[e % 2][:, tp * 66: tp * 66 + 64],
                        AF.Identity,
                        scale=rcb[:, (e % 2) * 4 + tp:(e % 2) * 4 + tp + 1],
                        bias=cst[:, C_ZERO:C_ZERO + 1])

            for h in range(H):
                hi = qc * H + h
                pr, hh = h // 2, h % 2
                base = hh * 64
                qslice = qbuf[base:base + 64, pr * QW:(pr + 1) * QW]
                if h == 0:
                    wait_tok(t_, q_tok)
                for j in range(16):
                    k = hi * 16 + j
                    if k >= 4:
                        t_.wait_ge(SC.c.sem, tok_exp[k - 4])
                    sems['sS'].inc(
                        t_.matmul(psS[k % 4][:, :],
                                  kbuf[base:base + 64,
                                       pr * 2048 + j * 128: pr * 2048 + j * 128 + 128],
                                  qslice, start=True, stop=True))
                    SC.wait_ge(sems['sS'].sem, k + 1)
                    if k >= 4:
                        SC.wait_ge(sems['sPV'].sem, k - 3)
                    tok_exp[k] = SC.op(
                        'activation', ptb[:, (k % 4) * QW:(k % 4) * QW + QW],
                        psS[k % 4][:, :], AF.Exp, bias=cst[:, C_NSHIFT:C_NSHIFT + 1])
                    if k - LAG >= qc * 256:
                        emit_pv(k - LAG)
                if h >= 1:
                    emit_epilogue(hi - 1)
                if h == H - 1:
                    for kk in range((qc + 1) * 256 - LAG, (qc + 1) * 256):
                        emit_pv(kk)
                    emit_epilogue(hi)

            t_.wait_ge(SC.c.sem, tok_ae[qc * H + H - 1])
            for tp in range(4):
                for dp in range(8):
                    k = qc * 32 + tp * 8 + dp
                    if k >= 2:
                        wait_tok(t_, (V.c, tok_fe[k - 2]))
                    bank, half = k % 2, (k // 2) % 2
                    sems['sFT'].inc(
                        t_.transpose(psS[bank][:, half * 128:half * 128 + 128],
                                     atm[:, tp * 1024 + dp * 128: tp * 1024 + dp * 128 + 128],
                                     ident[:]))
                    V.wait_ge(sems['sFT'].sem, sems['sFT'].n)
                    tok_fe[k] = V.op(
                        'tensor_copy', qbuf[:, dp * 512 + tp * 128: dp * 512 + tp * 128 + 128],
                        psS[bank][:, half * 128:half * 128 + 128])

            wait_tok(t_, (V.c, tok_fe[qc * 32 + 31]))
            def load_wout(gw, dp):
                if gw >= 2:
                    wait_tok(y_, swm_tok[gw - 2])
                wos_tok[gw] = streams['wo'].issue(
                    y_, [(woutb[:, (gw % 2) * 1024:(gw % 2) * 1024 + 1024],
                          wout_h[dp * 128:(dp + 1) * 128, :])])

            for tp in range(4):
                base_gw = (qc * 4 + tp) * 8
                for dp in range(2):
                    load_wout(base_gw + dp, dp)
                for nh in range(2):
                    u = (qc * 4 + tp) * 2 + nh
                    if u >= 2:
                        wait_tok(t_, (V.c, tok_ye[u - 2]))
                for dp in range(8):
                    gw = base_gw + dp
                    wait_tok(t_, wos_tok[gw])
                    for nh in range(2):
                        mm = t_.matmul(psY[nh][:, :],
                                       qbuf[:, dp * 512 + tp * 128: dp * 512 + tp * 128 + 128],
                                       woutb[:, (gw % 2) * 1024 + nh * 512:
                                             (gw % 2) * 1024 + nh * 512 + 512],
                                       start=(dp == 0), stop=(dp == 7))
                    sems['sWM'].inc(mm)
                    swm_tok[gw] = (sems['sWM'], sems['sWM'].n)
                    if dp + 2 < 8:
                        load_wout(base_gw + dp + 2, dp + 2)
                for nh in range(2):
                    u = (qc * 4 + tp) * 2 + nh
                    V.wait_ge(sems['sWM'].sem, sems['sWM'].n)
                    if tp == 0 and nh == 0:
                        wait_tok(V, x_tok)
                    tok_ye[u] = V.op(
                        'tensor_tensor',
                        atm[:, tp * 1024 + nh * 512: tp * 1024 + nh * 512 + 512],
                        xres[:, tp * 1024 + nh * 512: tp * 1024 + nh * 512 + 512],
                        psY[nh][:, :], op=ALU.add)
            g_.wait_ge(V.c.sem, tok_ye[(qc * 4 + 3) * 2 + 1])
            x1_toks.append(streams['x1st'].issue(
                g_, [(x1_r[qc, t], atm[:, t * 1024:(t + 1) * 1024]) for t in range(4)]))


def _phase3(ctx, x1_s, w1_h, b1_h, w2_h, b2_h, rw_h, out_h, stages=(), dbg=None):
    nc, sems, streams = ctx['nc'], ctx['sems'], ctx['streams']
    V, SC = ctx['V'], ctx['SC']
    ident, cst = ctx['ident'], ctx['cst']
    from contextlib import ExitStack
    es = ExitStack()
    ln2T = es.enter_context(nc.sbuf_tensor('p3_ln2T', [128, 8 * 2048], F32))
    yacc = es.enter_context(nc.sbuf_tensor('p3_yacc', [128, 16 * 1024], F32))
    x1t = es.enter_context(nc.sbuf_tensor('p3_x1t', [128, 2 * 1024], F32))
    ln2 = es.enter_context(nc.sbuf_tensor('p3_ln2', [128, 2 * 1024], F32))
    sqscr = es.enter_context(nc.sbuf_tensor('p3_sqscr', [128, 1024], F32))
    st = es.enter_context(nc.sbuf_tensor('p3_st', [128, 16], F32))
    rwb = es.enter_context(nc.sbuf_tensor('p3_rw', [128, 64], F32))
    rtb = es.enter_context(nc.sbuf_tensor('p3_rt', [128, 8], F32))
    rpb = es.enter_context(nc.sbuf_tensor('p3_rp', [128, 24], F32))
    fbuf = es.enter_context(nc.sbuf_tensor('p3_fb', [128, 16], F32))
    b1c = es.enter_context(nc.sbuf_tensor('p3_b1c', [128, 32], F32))
    b2b = es.enter_context(nc.sbuf_tensor('p3_b2b', [128, 1024], F32))
    w1s = es.enter_context(nc.sbuf_tensor('p3_w1s', [128, 2 * 1024], F32))
    w2s = es.enter_context(nc.sbuf_tensor('p3_w2s', [128, 2 * 1024], F32))
    hTb = es.enter_context(nc.sbuf_tensor('p3_hT', [128, 2 * 2048], F32))
    tmpb = es.enter_context(nc.sbuf_tensor('p3_tmp', [128, 1024], F32))
    rlog = es.enter_context(nc.sbuf_tensor('p3_rlog', [128, 128], F32))
    outb = es.enter_context(nc.sbuf_tensor('p3_out', [128, 2 * 1024], F32))

    t_, y_, g_ = nc.tensor, nc.sync, nc.gpsimd
    x1_r = x1_s.rearrange('(t p) d -> t p d', p=128)
    out_r = out_h.rearrange('(t p) d -> t p d', p=128)
    tp_base = sems['sTP'].n

    tok_ln = {}
    tok_le = {}
    tok_rexp = {}
    tok_fc = {}
    tok_ge = {}
    tok_ya = {}
    ost_toks = {}

    with es:
        small_tok = streams['const'].issue(
            y_, [(rwb[:, dp * 8:(dp + 1) * 8], rw_h[dp * 128:(dp + 1) * 128, :])
                 for dp in range(8)] + [(b1c[:], b1_h), (b2b[:], b2_h)])
        for c in streams['x1st'].sems:
            y_.wait_ge(c.sem, c.n)

        esA = ExitStack()
        ptr = esA.enter_context(nc.psum_tensor('p3_ptr', [128, 2 * 512], F32))
        plog = esA.enter_context(nc.psum_tensor('p3_plog', [128, 8], F32))
        with esA:
            for t in range(NT):
                if t >= 2:
                    wait_tok(y_, (SC.c, tok_ln[t - 2]))
                a_tok = streams['a'].issue(
                    y_, [(x1t[:, (t % 2) * 1024:(t % 2) * 1024 + 1024], x1_r[t])])
                xtile = x1t[:, (t % 2) * 1024:(t % 2) * 1024 + 1024]
                V.wait_tok(a_tok)
                SC.wait_tok(a_tok)
                _emit_stats(ctx, xtile, sqscr, st, rms=False)
                SC.wait_ge(V.c.sem, V.c.n)
                if t >= 2:
                    SC.wait_ge(sems['sTP'].sem, tp_base + 8 * (t - 1))
                tok_ln[t] = SC.op(
                    'activation', ln2[:, (t % 2) * 1024:(t % 2) * 1024 + 1024], xtile,
                    AF.Identity, bias=st[:, 11:12], scale=st[:, 10:11])
                t_.wait_ge(SC.c.sem, tok_ln[t])
                for dp in range(8):
                    k = 8 * t + dp
                    if k >= 2:
                        wait_tok(t_, (V.c, tok_le[k - 2]))
                    off = (k % 2) * 512 + ((k // 2) % 4) * 128
                    sems['sTP'].inc(
                        t_.transpose(ptr[:, off:off + 128],
                                     ln2[:, (t % 2) * 1024 + dp * 128:
                                         (t % 2) * 1024 + dp * 128 + 128],
                                     ident[:]))
                    V.wait_ge(sems['sTP'].sem, sems['sTP'].n)
                    tok_le[k] = V.op(
                        'tensor_copy', ln2T[:, dp * 2048 + t * 128: dp * 2048 + t * 128 + 128],
                        ptr[:, off:off + 128])
                wait_tok(t_, (V.c, tok_le[8 * t + 7]))
                if t == 0:
                    wait_tok(t_, small_tok)
                if t >= 1:
                    wait_tok(t_, (V.c, tok_rexp[t - 1]))
                for dp in range(8):
                    mm = t_.matmul(plog[:, :],
                                   ln2T[:, dp * 2048 + t * 128: dp * 2048 + t * 128 + 128],
                                   rwb[:, dp * 8:(dp + 1) * 8],
                                   start=(dp == 0), stop=(dp == 7))
                sems['sRT'].inc(mm)
                V.wait_ge(sems['sRT'].sem, sems['sRT'].n)
                tok_rexp[t] = V.op('tensor_copy', rlog[:, t * 8:(t + 1) * 8], plog[:, :])
            for t in range(NT):
                V.op('reduce_max', rtb[:, 0:1], rlog[:, t * 8:(t + 1) * 8], axis=AX.X)
                t_nm = V.op('tensor_scalar', rtb[:, 1:2], rtb[:, 0:1], -1.0, None,
                            op0=ALU.mult)
                SC.wait_ge(V.c.sem, t_nm)
                t_re = SC.op('activation', rpb[:, 0:8], rlog[:, t * 8:(t + 1) * 8],
                             AF.Exp, bias=rtb[:, 1:2])
                V.wait_ge(SC.c.sem, t_re)
                V.op('reduce_sum', rtb[:, 2:3], rpb[:, 0:8], axis=AX.X)
                V.op('reciprocal', rtb[:, 3:4], rtb[:, 2:3])
                V.op('tensor_scalar', rpb[:, 8:16], rpb[:, 0:8], rtb[:, 3:4], None,
                     op0=ALU.mult)
                V.op('reduce_max', rtb[:, 4:5], rpb[:, 8:16], axis=AX.X)
                V.op('tensor_scalar', rpb[:, 16:24], rpb[:, 8:16], rtb[:, 4:5], None,
                     op0=ALU.is_equal)
                V.op('scalar_tensor_tensor', rpb[:, 16:24], rpb[:, 16:24], -1e30,
                     rpb[:, 8:16], op0=ALU.mult, op1=ALU.add)
                V.op('reduce_max', rtb[:, 5:6], rpb[:, 16:24], axis=AX.X)
                V.op('tensor_tensor', rtb[:, 6:7], rtb[:, 4:5], rtb[:, 5:6], op=ALU.add)
                V.op('reciprocal', rtb[:, 7:8], rtb[:, 6:7])
                tok_fc[t] = V.op('tensor_tensor', fbuf[:, t:t + 1], rtb[:, 6:7],
                                 rtb[:, 7:8], op=ALU.mult)
        hsS_afterA = SC.c.n
        tok_lnT_all = tok_le[8 * NT - 8 + 7]
        if 'p3a' in stages:
            g_.wait_ge(V.c.sem, V.c.n)
            streams['dbg'].issue(g_, [(dbg['ln2T'], ln2T[:]), (dbg['fb'], fbuf[:])])
            return

        ph = [es.enter_context(nc.psum_tensor(f'p3_ph{i}', [128, 512], F32)) for i in range(2)]
        py = [es.enter_context(nc.psum_tensor(f'p3_py{i}', [128, 1024], F32)) for i in range(2)]
        w1_toks, w2_toks = {}, {}
        for fg in range(16):
            for fp2 in range(2):
                fp = 2 * fg + fp2
                if fp >= 2:
                    y_.wait_ge(sems['sHM'].sem, 4 * (fp - 1))
                w1_toks[fp] = streams['w1'].issue(
                    y_, [(w1s[:, (fp % 2) * 1024:(fp % 2) * 1024 + 1024],
                          w1_h[fp].rearrange('p dp j -> p (dp j)'))])
                if fg >= 1:
                    y_.wait_ge(sems['sWM2'].sem, 16 * fg)
                w2_toks[fp] = streams['w2'].issue(
                    y_, [(w2s[:, fp2 * 1024:fp2 * 1024 + 1024],
                          w2_h[fp * 128:(fp + 1) * 128, :])])
                for tc4 in range(4):
                    m = 4 * fp + tc4
                    wait_tok(t_, w1_toks[fp])
                    if fp == 0 and tc4 == 0:
                        wait_tok(t_, (V.c, tok_lnT_all))
                        t_.wait_ge(SC.c.sem, hsS_afterA)
                    if m >= 2:
                        wait_tok(t_, (SC.c, tok_ge[m - 2]))
                    for dp in range(8):
                        mm = t_.matmul(ph[m % 2][:, :],
                                       w1s[:, (fp % 2) * 1024 + dp * 128:
                                           (fp % 2) * 1024 + dp * 128 + 128],
                                       ln2T[:, dp * 2048 + tc4 * 512:
                                            dp * 2048 + tc4 * 512 + 512],
                                       start=(dp == 0), stop=(dp == 7))
                    sems['sHM'].inc(mm)
                    SC.wait_ge(sems['sHM'].sem, sems['sHM'].n)
                    if fg >= 1 and tc4 == 0 and fp2 == 0:
                        SC.wait_ge(sems['sWM2'].sem, 16 * fg)
                    tok_ge[m] = SC.op(
                        'activation', hTb[:, fp2 * 2048 + tc4 * 512:
                                          fp2 * 2048 + tc4 * 512 + 512],
                        ph[m % 2][:, :], AF.Gelu_apprx_tanh, bias=b1c[:, fp:fp + 1])
            for t16 in range(NT):
                w = 16 * fg + t16
                t_.wait_ge(SC.c.sem, tok_ge[4 * (2 * fg + 1) + 3])
                wait_tok(t_, w2_toks[2 * fg])
                wait_tok(t_, w2_toks[2 * fg + 1])
                if w >= 2:
                    wait_tok(t_, (V.c, tok_ya[w - 2]))
                for nh in range(2):
                    for fp2 in range(2):
                        mm = t_.matmul(py[w % 2][:, nh * 512:nh * 512 + 512],
                                       hTb[:, fp2 * 2048 + t16 * 128:
                                           fp2 * 2048 + t16 * 128 + 128],
                                       w2s[:, fp2 * 1024 + nh * 512:
                                           fp2 * 1024 + nh * 512 + 512],
                                       start=(fp2 == 0), stop=(fp2 == 1))
                sems['sWM2'].inc(mm)
                V.wait_ge(sems['sWM2'].sem, sems['sWM2'].n)
                if fg == 0:
                    tok_ya[w] = V.op('tensor_copy', yacc[:, t16 * 1024:(t16 + 1) * 1024],
                                     py[w % 2][:, :])
                else:
                    tok_ya[w] = V.op('tensor_tensor', yacc[:, t16 * 1024:(t16 + 1) * 1024],
                                     yacc[:, t16 * 1024:(t16 + 1) * 1024],
                                     py[w % 2][:, :], op=ALU.add)

        if 'p3b' in stages:
            g_.wait_ge(V.c.sem, V.c.n)
            streams['dbg'].issue(g_, [(dbg['yacc'], yacc[:])])
            return

        tok_oadd = {}
        tok_osc = {}
        for t in range(NT):
            if t >= 2:
                wait_tok(y_, (V.c, tok_oadd[t - 2]))
            c_tok = streams['a'].issue(
                y_, [(x1t[:, (t % 2) * 1024:(t % 2) * 1024 + 1024], x1_r[t])])
            if t >= 1:
                V.wait_ge(SC.c.sem, tok_osc[t - 1])
            t_tmp = V.op('tensor_tensor', tmpb[:], yacc[:, t * 1024:(t + 1) * 1024],
                         b2b[:], op=ALU.add)
            SC.wait_ge(V.c.sem, t_tmp)
            SC.wait_ge(V.c.sem, tok_fc[NT - 1])
            if t >= 2:
                wait_tok(SC, ost_toks[t - 2])
            tok_osc[t] = SC.op(
                'activation', outb[:, (t % 2) * 1024:(t % 2) * 1024 + 1024], tmpb[:],
                AF.Identity, scale=fbuf[:, t:t + 1], bias=cst[:, C_ZERO:C_ZERO + 1])
            V.wait_ge(SC.c.sem, tok_osc[t])
            V.wait_tok(c_tok)
            tok_oadd[t] = V.op(
                'tensor_tensor', outb[:, (t % 2) * 1024:(t % 2) * 1024 + 1024],
                outb[:, (t % 2) * 1024:(t % 2) * 1024 + 1024],
                x1t[:, (t % 2) * 1024:(t % 2) * 1024 + 1024], op=ALU.add)
            g_.wait_ge(V.c.sem, tok_oadd[t])
            ost_toks[t] = streams['ost'].issue(
                g_, [(out_r[t], outb[:, (t % 2) * 1024:(t % 2) * 1024 + 1024])])


_NC_CACHE = {}


def kernel(L0_attn_x, L0_attn_gamma, L0_attn_w_qkv, L0_attn_w_out,
           L0_ffn_router_w, L0_ffn_w1, L0_ffn_b1, L0_ffn_w2, L0_ffn_b2):
    from concourse.bass_utils import run_bass_kernel_spmd
    x_full = np.asarray(L0_attn_x, np.float32)
    n = 8
    base = host_inputs(x_full[0], L0_attn_gamma, L0_attn_w_qkv, L0_attn_w_out,
                       L0_ffn_router_w, L0_ffn_w1, L0_ffn_b1, L0_ffn_w2, L0_ffn_b2)
    in_maps = []
    for b in range(n):
        m = dict(base)
        m['x'] = np.ascontiguousarray(x_full[b])
        in_maps.append(m)
    if 'nc' not in _NC_CACHE:
        _NC_CACHE['nc'] = build()
    res = run_bass_kernel_spmd(_NC_CACHE['nc'], in_maps, core_ids=list(range(n)))
    out = np.stack([np.asarray(r['out'], np.float32) for r in res.results], axis=0)
    return out


# version 05 (final)
# speedup vs baseline: 1.6505x; 1.2362x over previous
import numpy as np
import concourse.bass as bass
import concourse.mybir as mybir

F32 = mybir.dt.float32
BF16 = mybir.dt.bfloat16
AF = mybir.ActivationFunctionType
ALU = mybir.AluOpType
AX = mybir.AxisListType

S = 2048
D = 1024
H = 16
F = 4096
NT = S // 128
NC4 = 4
QC = 8
VW = 66
SHIFT = 20.0
C_NSHIFT, C_QSCALE, C_ONE, C_ZERO = 0, 1, 2, 3


def host_inputs(x, gamma, wqkv, wout, router_w, w1, b1, w2, b2):
    d = {}
    d['x'] = np.ascontiguousarray(np.asarray(x, np.float32))
    d['gamma_c'] = np.ascontiguousarray(np.asarray(gamma, np.float32).reshape(8, 128).T)
    d['wqkv_bl'] = np.ascontiguousarray(
        np.asarray(wqkv, np.float32).reshape(8, 128, 24, 128).transpose(2, 1, 0, 3))
    d['wout'] = np.ascontiguousarray(np.asarray(wout, np.float32))
    d['rw'] = np.ascontiguousarray(np.asarray(router_w, np.float32))
    d['w1_bl'] = np.ascontiguousarray(
        np.asarray(w1, np.float32).reshape(8, 128, 32, 128).transpose(2, 1, 0, 3))
    d['b1_c'] = np.ascontiguousarray(np.asarray(b1, np.float32).reshape(32, 128).T)
    d['w2'] = np.ascontiguousarray(np.asarray(w2, np.float32))
    d['b2_bc'] = np.ascontiguousarray(np.broadcast_to(np.asarray(b2, np.float32), (128, D)))
    d['ident'] = np.eye(128, dtype=np.float32)
    cst = np.zeros((128, 4), np.float32)
    cst[:, C_NSHIFT] = -SHIFT
    cst[:, C_QSCALE] = 0.125
    cst[:, C_ONE] = 1.0
    d['cst'] = cst
    return d


class Ctr:
    def __init__(self, sem):
        self.sem = sem
        self.n = 0

    def inc(self, inst, by=1):
        inst.then_inc(self.sem, by)
        self.n += by
        return self.n

    def wait(self, eng, target=None):
        eng.wait_ge(self.sem, self.n if target is None else target)


class ChEng:

    def __init__(self, eng, ctr):
        self.e = eng
        self.c = ctr

    def op(self, name, *a, **k):
        self.e.wait_ge(self.c.sem, self.c.n)
        inst = getattr(self.e, name)(*a, **k)
        self.c.inc(inst)
        return self.c.n

    def wait_ge(self, sem, n):
        self.e.wait_ge(sem, n)

    def wait_tok(self, tok):
        c, n = tok
        self.e.wait_ge(c.sem, n)

    @property
    def tok(self):
        return (self.c, self.c.n)


class DmaStream:

    def __init__(self, nc, es, name, nbuf=1):
        self.sems = [Ctr(es.enter_context(nc.semaphore(f'{name}{i}'))) for i in range(nbuf)]
        self.nbuf = nbuf
        self.i = 0

    def issue(self, eng, pairs):
        c = self.sems[self.i % self.nbuf]
        eng.wait_ge(c.sem, c.n)
        for out, in_ in pairs:
            c.inc(eng.dma_start(out, in_), 16)
        self.i += 1
        return (c, c.n)


def wait_tok(eng, tok):
    c, n = tok
    eng.wait_ge(c.sem, n)


def build():
    phases = (1, 2, 3)
    debug = ()
    nc = bass.Bass('TRN2', target_bir_lowering=False)

    x_h = nc.dram_tensor('x', [S, D], F32, kind='ExternalInput').ap()
    gamma_h = nc.dram_tensor('gamma_c', [128, 8], F32, kind='ExternalInput').ap()
    wqkv_h = nc.dram_tensor('wqkv_bl', [24, 128, 8, 128], F32, kind='ExternalInput').ap()
    wout_h = nc.dram_tensor('wout', [D, D], F32, kind='ExternalInput').ap()
    rw_h = nc.dram_tensor('rw', [D, 8], F32, kind='ExternalInput').ap()
    w1_h = nc.dram_tensor('w1_bl', [32, 128, 8, 128], F32, kind='ExternalInput').ap()
    b1_h = nc.dram_tensor('b1_c', [128, 32], F32, kind='ExternalInput').ap()
    w2_h = nc.dram_tensor('w2', [F, D], F32, kind='ExternalInput').ap()
    b2_h = nc.dram_tensor('b2_bc', [128, D], F32, kind='ExternalInput').ap()
    ident_h = nc.dram_tensor('ident', [128, 128], F32, kind='ExternalInput').ap()
    cst_h = nc.dram_tensor('cst', [128, 4], F32, kind='ExternalInput').ap()

    qT_s = nc.dram_tensor('qT_s', [QC, 8, 128, 256], F32, kind='Internal').ap()
    x1_s = nc.dram_tensor('x1_s', [S, D], F32,
                          kind='ExternalInput' if 'x1in' in debug else 'Internal').ap()
    out_h = nc.dram_tensor('out', [S, D], F32, kind='ExternalOutput').ap()
    dbg = {}
    if 'p1' in debug:
        dbg['kT'] = nc.dram_tensor('dbg_kT', [128, 8 * 2048], F32, kind='ExternalOutput').ap()
        dbg['vst'] = nc.dram_tensor('dbg_vst', [128, NT * VW * H], F32, kind='ExternalOutput').ap()
        dbg['qT'] = nc.dram_tensor('dbg_qT', [QC, 8, 128, 256], F32, kind='ExternalOutput').ap()
    if 'p2' in debug:
        dbg['x1'] = nc.dram_tensor('dbg_x1', [S, D], F32, kind='ExternalOutput').ap()
    if 'p3a' in debug:
        dbg['ln2T'] = nc.dram_tensor('dbg_ln2T', [128, 8 * 2048], F32, kind='ExternalOutput').ap()
        dbg['fb'] = nc.dram_tensor('dbg_fb', [128, 16], F32, kind='ExternalOutput').ap()
    if 'p3b' in debug:
        dbg['yacc'] = nc.dram_tensor('dbg_yacc', [128, 16 * 1024], F32, kind='ExternalOutput').ap()

    from contextlib import ExitStack
    es = ExitStack()
    ident = es.enter_context(nc.sbuf_tensor('ident_sb', [128, 128], F32))
    cst = es.enter_context(nc.sbuf_tensor('cst_sb', [128, 4], F32))
    gam = es.enter_context(nc.sbuf_tensor('gam_sb', [128, 8], F32))
    gam1 = es.enter_context(nc.sbuf_tensor('gam1_sb', [128, 8], F32))
    es12 = ExitStack()
    kbuf = es12.enter_context(nc.sbuf_tensor('kbuf', [128, 8 * 2048], F32))
    vstore = es12.enter_context(nc.sbuf_tensor('vstore', [128, NT * VW * H], F32))

    sem_names = ['cV', 'cS', 'sTP', 'sTV', 'sMM', 'sS', 'sPV', 'sTT', 'sFT',
                 'sRT', 'sWM', 'sHM', 'sWM2', 'sBAR']
    sems = {name: Ctr(es.enter_context(nc.semaphore(name))) for name in sem_names}

    streams = {
        'const': DmaStream(nc, es, 'dCON', 1),
        'xld': DmaStream(nc, es, 'dX', 1),
        'w': DmaStream(nc, es, 'dW', 2),
        'qst': DmaStream(nc, es, 'dQS', 2),
        'ql': DmaStream(nc, es, 'dQL', 1),
        'xr2': DmaStream(nc, es, 'dXR', 1),
        'wo': DmaStream(nc, es, 'dWO', 1),
        'x1st': DmaStream(nc, es, 'dX1', 1),
        'a': DmaStream(nc, es, 'dA', 2),
        'w1': DmaStream(nc, es, 'dW1', 2),
        'w2': DmaStream(nc, es, 'dW2', 2),
        'ost': DmaStream(nc, es, 'dOS', 2),
        'dbg': DmaStream(nc, es, 'dDB', 1),
    }

    V = ChEng(nc.vector, sems['cV'])
    SC = ChEng(nc.scalar, sems['cS'])
    ctx = dict(nc=nc, sems=sems, streams=streams, V=V, SC=SC,
               kbuf=kbuf, vstore=vstore, ident=ident, cst=cst, gam=gam1)

    with es:
        tok_const = streams['const'].issue(
            nc.sync, [(ident[:], ident_h), (cst[:], cst_h), (gam[:], gamma_h)])
        vst_blk = vstore[:].rearrange('p (b w) -> p b w', w=VW)
        V.wait_tok(tok_const)
        V.op('memset', vst_blk[:, :, 64:65], 1.0)
        V.op('memset', vst_blk[:, :, 65:66], 0.0)
        vinit_tok = V.tok
        V.op('tensor_scalar', gam1[:], gam[:], 1.0, None, op0=ALU.add)
        for eng in (nc.tensor, nc.scalar, nc.sync, nc.gpsimd):
            wait_tok(eng, tok_const)
        nbar = 0

        if 1 in phases:
            _phase1(ctx, x_h, wqkv_h, qT_s, vinit_tok)
            for c in streams['qst'].sems:
                nc.gpsimd.wait_ge(c.sem, c.n)
            nbar += 1
            _barrier(nc, sems['sBAR'], nbar, chains=(V, SC))
        if 2 in phases:
            _phase2(ctx, x_h, wout_h, qT_s, x1_s)
            for c in streams['x1st'].sems:
                nc.gpsimd.wait_ge(c.sem, c.n)
            nbar += 1
            _barrier(nc, sems['sBAR'], nbar, chains=(V, SC))
        if 3 in phases:
            es12.close()
            _phase3(ctx, x1_s, w1_h, b1_h, w2_h, b2_h, rw_h, out_h,
                    stages=[d for d in debug if d in ('p3a', 'p3b')], dbg=dbg)
            nbar += 1
            _barrier(nc, sems['sBAR'], nbar, chains=(V, SC))

        if 3 not in phases:
            es12.close()
        pairs = []
        if 'p1' in debug:
            pairs += [(dbg['kT'], kbuf[:]), (dbg['vst'], vstore[:]), (dbg['qT'], qT_s)]
        if 'p2' in debug:
            pairs += [(dbg['x1'], x1_s)]
        if pairs:
            for stn in ('qst', 'x1st'):
                for c in streams[stn].sems:
                    nc.gpsimd.wait_ge(c.sem, c.n)
            streams['dbg'].issue(nc.gpsimd, pairs)
        for stn in ('ost', 'dbg'):
            for c in streams[stn].sems:
                nc.gpsimd.wait_ge(c.sem, c.n)
    return nc


def _barrier(nc, bar, k, chains=()):
    for c in chains:
        c.e.wait_ge(c.c.sem, c.c.n)
    for eng in (nc.tensor, nc.vector, nc.scalar, nc.sync, nc.gpsimd):
        bar.inc(eng.nop())
    for eng in (nc.tensor, nc.vector, nc.scalar, nc.sync, nc.gpsimd):
        eng.wait_ge(bar.sem, 5 * k)


def _emit_stats(ctx, xtile, sqscr, st, rms):
    V, SC = ctx['V'], ctx['SC']
    t_sq = SC.op('activation', sqscr[:], xtile, AF.Square, accum_out=st[:, 1:2])
    V.op('reduce_sum', st[:, 0:1], xtile, axis=AX.X)
    V.wait_ge(SC.c.sem, t_sq)
    V.op('tensor_scalar', st[:, 2:3], st[:, 0:1], 1.0 / D, None, op0=ALU.mult)
    V.op('tensor_tensor', st[:, 3:4], st[:, 2:3], st[:, 2:3], op=ALU.mult)
    V.op('tensor_scalar', st[:, 4:5], st[:, 1:2], 1.0 / D, None, op0=ALU.mult)
    V.op('tensor_tensor', st[:, 5:6], st[:, 4:5], st[:, 3:4], op=ALU.subtract)
    V.op('tensor_scalar', st[:, 6:7], st[:, 5:6], 1e-5, None, op0=ALU.add)
    t_rv = V.op('reciprocal', st[:, 7:8], st[:, 6:7])
    if rms:
        V.op('tensor_scalar', st[:, 8:9], st[:, 5:6], float(D), None, op0=ALU.mult)
        t_ss2 = V.op('tensor_tensor', st[:, 8:9], st[:, 8:9], st[:, 7:8], op=ALU.mult)
        SC.wait_ge(V.c.sem, t_ss2)
        t_sqr = SC.op('activation', st[:, 12:14], st[:, 7:9], AF.Sqrt)
        V.wait_ge(SC.c.sem, t_sqr)
        V.op('tensor_scalar', st[:, 9:10], st[:, 13:14], 1e-12, None, op0=ALU.max)
        V.op('reciprocal', st[:, 9:10], st[:, 9:10])
        V.op('tensor_scalar', st[:, 10:11], st[:, 12:13], st[:, 9:10], 32.0,
             op0=ALU.mult, op1=ALU.mult)
    else:
        SC.wait_ge(V.c.sem, t_rv)
        t_sqr = SC.op('activation', st[:, 10:11], st[:, 7:8], AF.Sqrt)
        V.wait_ge(SC.c.sem, t_sqr)
    t_b = V.op('tensor_scalar', st[:, 11:12], st[:, 2:3], st[:, 10:11], -1.0,
               op0=ALU.mult, op1=ALU.mult)
    return t_b


def _phase1(ctx, x_h, wqkv_h, qT_s, vinit_tok):
    nc, sems, streams = ctx['nc'], ctx['sems'], ctx['streams']
    V, SC = ctx['V'], ctx['SC']
    kbuf, vstore, ident, cst, gam = (ctx[k] for k in ('kbuf', 'vstore', 'ident', 'cst', 'gam'))
    from contextlib import ExitStack
    es = ExitStack()
    xbuf = es.enter_context(nc.sbuf_tensor('p1_xbuf', [128, 2 * 1024], F32))
    xr = es.enter_context(nc.sbuf_tensor('p1_xr', [128, 2 * 1024], F32))
    sqscr = es.enter_context(nc.sbuf_tensor('p1_sqscr', [128, 1024], F32))
    st = es.enter_context(nc.sbuf_tensor('p1_st', [128, 16], F32))
    xtbh = es.enter_context(nc.sbuf_tensor('p1_xtbh', [128, 8 * 512], BF16))
    xtbl = es.enter_context(nc.sbuf_tensor('p1_xtbl', [128, 8 * 512], BF16))
    wstrip = es.enter_context(nc.sbuf_tensor('p1_wstrip', [128, 2 * 1024], F32))
    wh = es.enter_context(nc.sbuf_tensor('p1_wh', [128, 2 * 1024], BF16))
    wl = es.enter_context(nc.sbuf_tensor('p1_wl', [128, 2 * 1024], BF16))
    qstage = es.enter_context(nc.sbuf_tensor('p1_qstage', [128, 2 * 512], F32))
    vtbuf = es.enter_context(nc.sbuf_tensor('p1_vtbuf', [128, 8 * 512], F32))
    ptp = es.enter_context(nc.psum_tensor('p1_ptp', [128, 2 * 512], F32))
    pqkv = es.enter_context(nc.psum_tensor('p1_pqkv', [128, 2 * 512], F32))
    pvt = es.enter_context(nc.psum_tensor('p1_pvt', [128, 2 * 512], F32))

    x_r = x_h.rearrange('(c t p) d -> c t p d', t=4, p=128)
    t_, y_, g_ = nc.tensor, nc.sync, nc.gpsimd
    strip_done = {}
    strip_ld = {}
    qst_tok = {}
    tok_xr = {}
    tok_xev = {}
    tok_xtb = {}
    tok_vcp = {}
    tok_vev = {}

    with es:
        for tc in range(NC4):
            if tc == 0:
                for g in range(2):
                    strip_ld[g] = streams['w'].issue(
                        y_, [(wstrip[:, (g % 2) * 1024:(g % 2) * 1024 + 1024],
                              wqkv_h[g % 24].rearrange('p dp j -> p (dp j)'))])

            for t in range(4):
                idx = 4 * tc + t
                if t % 2 == 0:
                    if idx >= 2:
                        wait_tok(y_, (SC.c, tok_xr[idx - 1]))
                    xtok = streams['xld'].issue(
                        y_, [(xbuf[:, tt * 1024:(tt + 1) * 1024], x_r[tc, t + tt])
                             for tt in range(2)])
                xtile = xbuf[:, (t % 2) * 1024:(t % 2) * 1024 + 1024]
                V.wait_tok(xtok)
                SC.wait_tok(xtok)
                _emit_stats(ctx, xtile, sqscr, st, rms=True)
                SC.wait_ge(V.c.sem, V.c.n)
                if idx >= 2:
                    SC.wait_ge(sems['sTP'].sem, 8 * (idx - 1))
                tok_xr[idx] = SC.op(
                    'activation', xr[:, (idx % 2) * 1024:(idx % 2) * 1024 + 1024],
                    xtile, AF.Identity, bias=st[:, 11:12], scale=st[:, 10:11])
                t_.wait_ge(SC.c.sem, tok_xr[idx])
                if t == 0 and tc > 0:
                    t_.wait_ge(sems['sMM'].sem, 24 * tc)
                if t == 0 and tc > 0:
                    V.wait_ge(sems['sMM'].sem, 24 * tc)
                for dp in range(8):
                    k = 8 * idx + dp
                    if k >= 2:
                        t_.wait_ge(V.c.sem, tok_xev[k - 2])
                    off = (k % 2) * 512 + ((k // 2) % 4) * 128
                    sems['sTP'].inc(
                        t_.transpose(ptp[:, off:off + 128],
                                     xr[:, (idx % 2) * 1024 + dp * 128:
                                        (idx % 2) * 1024 + dp * 128 + 128],
                                     ident[:]))
                    V.wait_ge(sems['sTP'].sem, sems['sTP'].n)
                    V.op('tensor_scalar',
                         xtbh[:, dp * 512 + t * 128: dp * 512 + t * 128 + 128],
                         ptp[:, off:off + 128],
                         gam[:, dp:dp + 1], None, op0=ALU.mult)
                    tok_xev[k] = V.op(
                        'scalar_tensor_tensor',
                        xtbl[:, dp * 512 + t * 128: dp * 512 + t * 128 + 128],
                        ptp[:, off:off + 128], gam[:, dp:dp + 1],
                        xtbh[:, dp * 512 + t * 128: dp * 512 + t * 128 + 128],
                        op0=ALU.mult, op1=ALU.subtract)
            tok_xtb[tc] = V.c.n

            for fp in range(24):
                g = 24 * tc + fp
                par = fp % 2
                wait_tok(V, strip_ld[g])
                if g >= 2:
                    wait_tok(V, strip_done[g - 2])
                V.op('tensor_copy', wh[:, par * 1024:par * 1024 + 1024],
                     wstrip[:, par * 1024:par * 1024 + 1024])
                tok_wsp = V.op('tensor_tensor', wl[:, par * 1024:par * 1024 + 1024],
                               wstrip[:, par * 1024:par * 1024 + 1024],
                               wh[:, par * 1024:par * 1024 + 1024], op=ALU.subtract)
                t_.wait_ge(V.c.sem, tok_wsp)
                if fp == 0:
                    t_.wait_ge(V.c.sem, tok_xtb[tc])
                if g >= 2:
                    wait_tok(t_, strip_done[g - 2])
                first = True
                for dp in range(8):
                    whp = wh[:, par * 1024 + dp * 128: par * 1024 + dp * 128 + 128]
                    wlp = wl[:, par * 1024 + dp * 128: par * 1024 + dp * 128 + 128]
                    xh = xtbh[:, dp * 512:(dp + 1) * 512]
                    xl = xtbl[:, dp * 512:(dp + 1) * 512]
                    for wop, xop, last in ((whp, xh, False), (whp, xl, False),
                                           (wlp, xh, dp == 7)):
                        mm = t_.matmul(pqkv[:, par * 512:par * 512 + 512],
                                       wop, xop, start=first, stop=last)
                        first = False
                sems['sMM'].inc(mm)

                if fp < 8:
                    qg = 8 * tc + fp
                    SC.wait_ge(sems['sMM'].sem, sems['sMM'].n)
                    if qg >= 2:
                        SC.wait_tok(qst_tok[qg - 2])
                    tq = SC.op('activation',
                               qstage[:, (qg % 2) * 512:(qg % 2) * 512 + 512],
                               pqkv[:, par * 512:par * 512 + 512],
                               AF.Identity, scale=cst[:, C_QSCALE:C_QSCALE + 1],
                               bias=cst[:, C_ZERO:C_ZERO + 1])
                    strip_done[g] = (SC.c, tq)
                    g_.wait_ge(SC.c.sem, tq)
                    qst_tok[qg] = streams['qst'].issue(
                        g_, [(qT_s[2 * tc + half, fp],
                              qstage[:, (qg % 2) * 512 + half * 256:
                                     (qg % 2) * 512 + half * 256 + 256])
                             for half in range(2)])
                elif fp < 16:
                    pr = fp - 8
                    V.wait_ge(sems['sMM'].sem, sems['sMM'].n)
                    tk = V.op('tensor_copy',
                              kbuf[:, pr * 2048 + tc * 512: pr * 2048 + tc * 512 + 512],
                              pqkv[:, par * 512:par * 512 + 512])
                    strip_done[g] = (V.c, tk)
                else:
                    vp = fp - 16
                    V.wait_ge(sems['sMM'].sem, sems['sMM'].n)
                    if vp == 0 and tc > 0:
                        V.wait_ge(sems['sTV'].sem, 32 * tc)
                    tv = V.op('tensor_copy', vtbuf[:, vp * 512:(vp + 1) * 512],
                              pqkv[:, par * 512:par * 512 + 512])
                    strip_done[g] = (V.c, tv)
                gn = g + 2
                if gn < 24 * NC4:
                    wait_tok(y_, strip_done[g])
                    strip_ld[gn] = streams['w'].issue(
                        y_, [(wstrip[:, (gn % 2) * 1024:(gn % 2) * 1024 + 1024],
                              wqkv_h[gn % 24].rearrange('p dp j -> p (dp j)'))])
            tok_vev[tc] = V.c.n

            t_.wait_ge(V.c.sem, tok_vev[tc])
            for vp in range(8):
                for i in range(4):
                    k = (8 * tc + vp) * 4 + i
                    jt = 4 * tc + i
                    if k >= 2:
                        t_.wait_ge(V.c.sem, tok_vcp[k - 2])
                    off = (k % 2) * 512 + ((k // 2) % 4) * 128
                    sems['sTV'].inc(
                        t_.transpose(pvt[:, off:off + 128],
                                     vtbuf[:, vp * 512 + i * 128: vp * 512 + i * 128 + 128],
                                     ident[:]))
                    V.wait_ge(sems['sTV'].sem, sems['sTV'].n)
                    if k == 0:
                        V.wait_tok(vinit_tok)
                    for hh in range(2):
                        h = 2 * vp + hh
                        tok_vcp[k] = V.op(
                            'tensor_copy',
                            vstore[:, jt * VW * H + h * VW: jt * VW * H + h * VW + 64],
                            pvt[:, off + hh * 64: off + hh * 64 + 64])


def _phase2(ctx, x_h, wout_h, qT_s, x1_s):
    nc, sems, streams = ctx['nc'], ctx['sems'], ctx['streams']
    V, SC = ctx['V'], ctx['SC']
    kbuf, vstore, ident, cst = (ctx[k] for k in ('kbuf', 'vstore', 'ident', 'cst'))
    from contextlib import ExitStack
    es = ExitStack()
    QW = 512
    NQ = S // QW
    woutb = es.enter_context(nc.sbuf_tensor('p2_wout', [128, 2 * 1024], F32))
    qbuf = es.enter_context(nc.sbuf_tensor('p2_qbuf', [128, 8 * QW], F32))
    xres = es.enter_context(nc.sbuf_tensor('p2_xres', [128, 4 * 1024], F32))
    ptb = es.enter_context(nc.sbuf_tensor('p2_ptb', [128, 8 * QW], F32))
    apv = es.enter_context(nc.sbuf_tensor('p2_apv', [128, 2 * QW], F32))
    atm = es.enter_context(nc.sbuf_tensor('p2_atm', [128, 4 * 1024], F32))
    rcb = es.enter_context(nc.sbuf_tensor('p2_rcb', [128, 8], F32))
    psS = [es.enter_context(nc.psum_tensor(f'p2_psS{i}', [128, QW], F32)) for i in range(4)]
    psPV = [es.enter_context(nc.psum_tensor(f'p2_psPV{i}', [128, QW], F32)) for i in range(2)]
    psY = [es.enter_context(nc.psum_tensor(f'p2_psY{i}', [128, 512], F32)) for i in range(2)]
    psT = [psY[0], psY[1]]

    t_, y_, g_ = nc.tensor, nc.sync, nc.gpsimd
    x_r = x_h.rearrange('(c t p) d -> c t p d', t=4, p=128)
    x1_r = x1_s.rearrange('(c t p) d -> c t p d', t=4, p=128)

    tok_exp = {}
    tok_apv = {}
    tok_rc = {}
    tok_ae = {}
    tok_fe = {}
    tok_ye = {}
    x1_toks = []
    wos_tok = {}
    swm_tok = {}

    with es:
        for c in streams['qst'].sems:
            y_.wait_ge(c.sem, c.n)

        for qc in range(NQ):
            if qc > 0:
                y_.wait_ge(sems['sWM'].sem, 32 * qc)
            q_tok = streams['ql'].issue(
                y_, [(qbuf[:, (2 * f + half) * 256:(2 * f + half) * 256 + 256],
                      qT_s[2 * qc + half, f])
                     for f in range(8) for half in range(2)])
            if qc > 0:
                wait_tok(y_, (V.c, tok_ye[8 * qc - 1]))
            x_tok = streams['xr2'].issue(
                y_, [(xres[:, t * 1024:(t + 1) * 1024], x_r[qc, t]) for t in range(4)])

            BURST = 4
            PVLAG = 2

            def emit_pv_burst(B):
                for i in range(BURST):
                    kk = 4 * B + i
                    hj, jj = divmod(kk, 16)
                    hh_ = hj % H
                    t_.wait_ge(SC.c.sem, tok_exp[kk])
                    if jj == 0 and hj >= 2:
                        wait_tok(t_, (V.c, tok_apv[hj - 2]))
                    sems['sPV'].inc(
                        t_.matmul(psPV[hj % 2][0:VW, :],
                                  vstore[:, jj * VW * H + hh_ * VW:
                                         jj * VW * H + hh_ * VW + VW],
                                  ptb[:, (kk % 8) * QW:(kk % 8) * QW + QW],
                                  start=(jj == 0), stop=(jj == 15)))

            def emit_epilogue(e):
                he = e % H
                V.wait_ge(sems['sPV'].sem, 16 * (e + 1))
                if e >= 2:
                    V.wait_ge(sems['sTT'].sem, 4 * (e - 1))
                tok_apv[e] = V.op(
                    'tensor_copy', apv[0:VW, (e % 2) * QW:(e % 2) * QW + QW],
                    psPV[e % 2][0:VW, :])
                t_.wait_ge(V.c.sem, tok_apv[e])
                if e >= 2:
                    t_.wait_ge(SC.c.sem, tok_ae[e - 2])
                if he <= 1 and qc > 0:
                    wait_tok(t_, (V.c, tok_ye[8 * qc - 1]))
                for tp in range(4):
                    sems['sTT'].inc(
                        t_.transpose(psT[e % 2][:, tp * 66:tp * 66 + 66],
                                     apv[0:VW, (e % 2) * QW + tp * 128:
                                         (e % 2) * QW + tp * 128 + 128],
                                     ident[0:VW, 0:VW]))
                V.wait_ge(sems['sTT'].sem, 4 * (e + 1))
                for tp in range(4):
                    tok_rc[e] = V.op(
                        'reciprocal', rcb[:, (e % 2) * 4 + tp:(e % 2) * 4 + tp + 1],
                        psT[e % 2][:, tp * 66 + 64: tp * 66 + 65])
                SC.wait_ge(V.c.sem, tok_rc[e])
                if he == 0 and qc > 0:
                    wait_tok(SC, x1_toks[qc - 1])
                for tp in range(4):
                    tok_ae[e] = SC.op(
                        'activation', atm[:, tp * 1024 + he * 64: tp * 1024 + he * 64 + 64],
                        psT[e % 2][:, tp * 66: tp * 66 + 64],
                        AF.Identity,
                        scale=rcb[:, (e % 2) * 4 + tp:(e % 2) * 4 + tp + 1],
                        bias=cst[:, C_ZERO:C_ZERO + 1])

            B0 = qc * 64
            for h in range(H):
                hi = qc * H + h
                pr, hh = h // 2, h % 2
                base = hh * 64
                qslice = qbuf[base:base + 64, pr * QW:(pr + 1) * QW]
                if h == 0:
                    wait_tok(t_, q_tok)
                for b in range(4):
                    B = hi * 4 + b
                    for i in range(BURST):
                        k = 4 * B + i
                        j = 4 * b + i
                        if k >= 4:
                            t_.wait_ge(SC.c.sem, tok_exp[k - 4])
                        sems['sS'].inc(
                            t_.matmul(psS[k % 4][:, :],
                                      kbuf[base:base + 64,
                                           pr * 2048 + j * 128: pr * 2048 + j * 128 + 128],
                                      qslice, start=True, stop=True))
                    for i in range(BURST):
                        k = 4 * B + i
                        SC.wait_ge(sems['sS'].sem, k + 1)
                        if k >= 8:
                            SC.wait_ge(sems['sPV'].sem, k - 7)
                        tok_exp[k] = SC.op(
                            'activation', ptb[:, (k % 8) * QW:(k % 8) * QW + QW],
                            psS[k % 4][:, :], AF.Exp,
                            bias=cst[:, C_NSHIFT:C_NSHIFT + 1])
                    if B - PVLAG >= B0:
                        emit_pv_burst(B - PVLAG)
                if h >= 1:
                    emit_epilogue(hi - 1)
                if h == H - 1:
                    for Bf in range((hi + 1) * 4 - PVLAG, (hi + 1) * 4):
                        emit_pv_burst(Bf)
                    emit_epilogue(hi)

            t_.wait_ge(SC.c.sem, tok_ae[qc * H + H - 1])
            for tp in range(4):
                for dp in range(8):
                    k = qc * 32 + tp * 8 + dp
                    if k >= 2:
                        wait_tok(t_, (V.c, tok_fe[k - 2]))
                    bank, half = k % 2, (k // 2) % 2
                    sems['sFT'].inc(
                        t_.transpose(psS[bank][:, half * 128:half * 128 + 128],
                                     atm[:, tp * 1024 + dp * 128: tp * 1024 + dp * 128 + 128],
                                     ident[:]))
                    V.wait_ge(sems['sFT'].sem, sems['sFT'].n)
                    tok_fe[k] = V.op(
                        'tensor_copy', qbuf[:, dp * 512 + tp * 128: dp * 512 + tp * 128 + 128],
                        psS[bank][:, half * 128:half * 128 + 128])

            wait_tok(t_, (V.c, tok_fe[qc * 32 + 31]))
            def load_wout(gw, dp):
                if gw >= 2:
                    wait_tok(y_, swm_tok[gw - 2])
                wos_tok[gw] = streams['wo'].issue(
                    y_, [(woutb[:, (gw % 2) * 1024:(gw % 2) * 1024 + 1024],
                          wout_h[dp * 128:(dp + 1) * 128, :])])

            for tp in range(4):
                base_gw = (qc * 4 + tp) * 8
                for dp in range(2):
                    load_wout(base_gw + dp, dp)
                for nh in range(2):
                    u = (qc * 4 + tp) * 2 + nh
                    if u >= 2:
                        wait_tok(t_, (V.c, tok_ye[u - 2]))
                for dp in range(8):
                    gw = base_gw + dp
                    wait_tok(t_, wos_tok[gw])
                    for nh in range(2):
                        mm = t_.matmul(psY[nh][:, :],
                                       qbuf[:, dp * 512 + tp * 128: dp * 512 + tp * 128 + 128],
                                       woutb[:, (gw % 2) * 1024 + nh * 512:
                                             (gw % 2) * 1024 + nh * 512 + 512],
                                       start=(dp == 0), stop=(dp == 7))
                    sems['sWM'].inc(mm)
                    swm_tok[gw] = (sems['sWM'], sems['sWM'].n)
                    if dp + 2 < 8:
                        load_wout(base_gw + dp + 2, dp + 2)
                for nh in range(2):
                    u = (qc * 4 + tp) * 2 + nh
                    V.wait_ge(sems['sWM'].sem, sems['sWM'].n)
                    if tp == 0 and nh == 0:
                        wait_tok(V, x_tok)
                    tok_ye[u] = V.op(
                        'tensor_tensor',
                        atm[:, tp * 1024 + nh * 512: tp * 1024 + nh * 512 + 512],
                        xres[:, tp * 1024 + nh * 512: tp * 1024 + nh * 512 + 512],
                        psY[nh][:, :], op=ALU.add)
            g_.wait_ge(V.c.sem, tok_ye[(qc * 4 + 3) * 2 + 1])
            x1_toks.append(streams['x1st'].issue(
                g_, [(x1_r[qc, t], atm[:, t * 1024:(t + 1) * 1024]) for t in range(4)]))


def _phase3(ctx, x1_s, w1_h, b1_h, w2_h, b2_h, rw_h, out_h, stages=(), dbg=None):
    nc, sems, streams = ctx['nc'], ctx['sems'], ctx['streams']
    V, SC = ctx['V'], ctx['SC']
    ident, cst = ctx['ident'], ctx['cst']
    from contextlib import ExitStack
    es = ExitStack()
    lnh = es.enter_context(nc.sbuf_tensor('p3_lnh', [128, 8 * 2048], BF16))
    lnl = es.enter_context(nc.sbuf_tensor('p3_lnl', [128, 8 * 2048], BF16))
    yacc = es.enter_context(nc.sbuf_tensor('p3_yacc', [128, 16 * 1024], F32))
    x1t = es.enter_context(nc.sbuf_tensor('p3_x1t', [128, 2 * 1024], F32))
    ln2 = es.enter_context(nc.sbuf_tensor('p3_ln2', [128, 2 * 1024], F32))
    sqscr = es.enter_context(nc.sbuf_tensor('p3_sqscr', [128, 1024], F32))
    st = es.enter_context(nc.sbuf_tensor('p3_st', [128, 16], F32))
    rwb = es.enter_context(nc.sbuf_tensor('p3_rw', [128, 64], F32))
    rtb = es.enter_context(nc.sbuf_tensor('p3_rt', [128, 8], F32))
    rpb = es.enter_context(nc.sbuf_tensor('p3_rp', [128, 24], F32))
    fbuf = es.enter_context(nc.sbuf_tensor('p3_fb', [128, 16], F32))
    b1c = es.enter_context(nc.sbuf_tensor('p3_b1c', [128, 32], F32))
    b2b = es.enter_context(nc.sbuf_tensor('p3_b2b', [128, 1024], F32))
    w1s = es.enter_context(nc.sbuf_tensor('p3_w1s', [128, 2 * 1024], F32))
    w1h = es.enter_context(nc.sbuf_tensor('p3_w1h', [128, 2 * 1024], BF16))
    w1l = es.enter_context(nc.sbuf_tensor('p3_w1l', [128, 2 * 1024], BF16))
    w2s = es.enter_context(nc.sbuf_tensor('p3_w2s', [128, 2 * 1024], F32))
    w2h = es.enter_context(nc.sbuf_tensor('p3_w2h', [128, 2 * 1024], BF16))
    w2l = es.enter_context(nc.sbuf_tensor('p3_w2l', [128, 2 * 1024], BF16))
    hTh = es.enter_context(nc.sbuf_tensor('p3_hTh', [128, 2 * 2048], BF16))
    hTl = es.enter_context(nc.sbuf_tensor('p3_hTl', [128, 2 * 2048], BF16))
    gtmp = es.enter_context(nc.sbuf_tensor('p3_gtmp', [128, 2 * 512], F32))
    rwh = es.enter_context(nc.sbuf_tensor('p3_rwh', [128, 64], BF16))
    rwl = es.enter_context(nc.sbuf_tensor('p3_rwl', [128, 64], BF16))
    rlog = es.enter_context(nc.sbuf_tensor('p3_rlog', [128, 128], F32))
    outb = ln2
    tmpb = sqscr

    t_, y_, g_ = nc.tensor, nc.sync, nc.gpsimd
    x1_r = x1_s.rearrange('(t p) d -> t p d', p=128)
    out_r = out_h.rearrange('(t p) d -> t p d', p=128)
    tp_base = sems['sTP'].n

    tok_ln = {}
    tok_le = {}
    tok_rexp = {}
    tok_fc = {}
    tok_ge = {}
    tok_ya = {}
    ost_toks = {}

    with es:
        small_tok = streams['const'].issue(
            y_, [(rwb[:, dp * 8:(dp + 1) * 8], rw_h[dp * 128:(dp + 1) * 128, :])
                 for dp in range(8)] + [(b1c[:], b1_h), (b2b[:], b2_h)])
        V.wait_tok(small_tok)
        V.op('tensor_copy', rwh[:], rwb[:])
        tok_rws = V.op('tensor_tensor', rwl[:], rwb[:], rwh[:], op=ALU.subtract)
        for c in streams['x1st'].sems:
            y_.wait_ge(c.sem, c.n)

        esA = ExitStack()
        ptr = esA.enter_context(nc.psum_tensor('p3_ptr', [128, 2 * 512], F32))
        plog = esA.enter_context(nc.psum_tensor('p3_plog', [128, 8], F32))
        with esA:
            for t in range(NT):
                if t >= 2:
                    wait_tok(y_, (SC.c, tok_ln[t - 2]))
                a_tok = streams['a'].issue(
                    y_, [(x1t[:, (t % 2) * 1024:(t % 2) * 1024 + 1024], x1_r[t])])
                xtile = x1t[:, (t % 2) * 1024:(t % 2) * 1024 + 1024]
                V.wait_tok(a_tok)
                SC.wait_tok(a_tok)
                _emit_stats(ctx, xtile, sqscr, st, rms=False)
                SC.wait_ge(V.c.sem, V.c.n)
                if t >= 2:
                    SC.wait_ge(sems['sTP'].sem, tp_base + 8 * (t - 1))
                tok_ln[t] = SC.op(
                    'activation', ln2[:, (t % 2) * 1024:(t % 2) * 1024 + 1024], xtile,
                    AF.Identity, bias=st[:, 11:12], scale=st[:, 10:11])
                t_.wait_ge(SC.c.sem, tok_ln[t])
                for dp in range(8):
                    k = 8 * t + dp
                    if k >= 2:
                        wait_tok(t_, (V.c, tok_le[k - 2]))
                    off = (k % 2) * 512 + ((k // 2) % 4) * 128
                    sems['sTP'].inc(
                        t_.transpose(ptr[:, off:off + 128],
                                     ln2[:, (t % 2) * 1024 + dp * 128:
                                         (t % 2) * 1024 + dp * 128 + 128],
                                     ident[:]))
                    V.wait_ge(sems['sTP'].sem, sems['sTP'].n)
                    V.op('tensor_copy', lnh[:, dp * 2048 + t * 128: dp * 2048 + t * 128 + 128],
                         ptr[:, off:off + 128])
                    tok_le[k] = V.op(
                        'tensor_tensor', lnl[:, dp * 2048 + t * 128: dp * 2048 + t * 128 + 128],
                        ptr[:, off:off + 128],
                        lnh[:, dp * 2048 + t * 128: dp * 2048 + t * 128 + 128],
                        op=ALU.subtract)
                wait_tok(t_, (V.c, tok_le[8 * t + 7]))
                if t == 0:
                    wait_tok(t_, small_tok)
                if t == 0:
                    t_.wait_ge(V.c.sem, tok_rws)
                if t >= 1:
                    wait_tok(t_, (V.c, tok_rexp[t - 1]))
                first = True
                for dp in range(8):
                    lh = lnh[:, dp * 2048 + t * 128: dp * 2048 + t * 128 + 128]
                    ll = lnl[:, dp * 2048 + t * 128: dp * 2048 + t * 128 + 128]
                    for a, bb, last in ((lh, rwh, False), (lh, rwl, False),
                                        (ll, rwh, dp == 7)):
                        mm = t_.matmul(plog[:, :], a, bb[:, dp * 8:(dp + 1) * 8],
                                       start=first, stop=last)
                        first = False
                sems['sRT'].inc(mm)
                V.wait_ge(sems['sRT'].sem, sems['sRT'].n)
                tok_rexp[t] = V.op('tensor_copy', rlog[:, t * 8:(t + 1) * 8], plog[:, :])
            for t in range(NT):
                V.op('reduce_max', rtb[:, 0:1], rlog[:, t * 8:(t + 1) * 8], axis=AX.X)
                t_nm = V.op('tensor_scalar', rtb[:, 1:2], rtb[:, 0:1], -1.0, None,
                            op0=ALU.mult)
                SC.wait_ge(V.c.sem, t_nm)
                t_re = SC.op('activation', rpb[:, 0:8], rlog[:, t * 8:(t + 1) * 8],
                             AF.Exp, bias=rtb[:, 1:2])
                V.wait_ge(SC.c.sem, t_re)
                V.op('reduce_sum', rtb[:, 2:3], rpb[:, 0:8], axis=AX.X)
                V.op('reciprocal', rtb[:, 3:4], rtb[:, 2:3])
                V.op('tensor_scalar', rpb[:, 8:16], rpb[:, 0:8], rtb[:, 3:4], None,
                     op0=ALU.mult)
                V.op('reduce_max', rtb[:, 4:5], rpb[:, 8:16], axis=AX.X)
                V.op('tensor_scalar', rpb[:, 16:24], rpb[:, 8:16], rtb[:, 4:5], None,
                     op0=ALU.is_equal)
                V.op('scalar_tensor_tensor', rpb[:, 16:24], rpb[:, 16:24], -1e30,
                     rpb[:, 8:16], op0=ALU.mult, op1=ALU.add)
                V.op('reduce_max', rtb[:, 5:6], rpb[:, 16:24], axis=AX.X)
                V.op('tensor_tensor', rtb[:, 6:7], rtb[:, 4:5], rtb[:, 5:6], op=ALU.add)
                V.op('reciprocal', rtb[:, 7:8], rtb[:, 6:7])
                tok_fc[t] = V.op('tensor_tensor', fbuf[:, t:t + 1], rtb[:, 6:7],
                                 rtb[:, 7:8], op=ALU.mult)
        hsS_afterA = SC.c.n
        tok_lnT_all = tok_le[8 * NT - 8 + 7]
        if 'p3a' in stages:
            g_.wait_ge(V.c.sem, V.c.n)
            streams['dbg'].issue(g_, [(dbg['ln2T'], ln2T[:]), (dbg['fb'], fbuf[:])])
            return

        ph = [es.enter_context(nc.psum_tensor(f'p3_ph{i}', [128, 512], F32)) for i in range(2)]
        py = [es.enter_context(nc.psum_tensor(f'p3_py{i}', [128, 1024], F32)) for i in range(2)]
        w1_toks, w2_toks = {}, {}
        tok_gsp = {}
        for fg in range(16):
            for fp2 in range(2):
                fp = 2 * fg + fp2
                if fp >= 2:
                    y_.wait_ge(sems['sHM'].sem, 4 * (fp - 1))
                w1_toks[fp] = streams['w1'].issue(
                    y_, [(w1s[:, (fp % 2) * 1024:(fp % 2) * 1024 + 1024],
                          w1_h[fp].rearrange('p dp j -> p (dp j)'))])
                if fg >= 1:
                    y_.wait_ge(sems['sWM2'].sem, 16 * fg)
                w2_toks[fp] = streams['w2'].issue(
                    y_, [(w2s[:, fp2 * 1024:fp2 * 1024 + 1024],
                          w2_h[fp * 128:(fp + 1) * 128, :])])
                wait_tok(V, w1_toks[fp])
                if fp >= 2:
                    V.wait_ge(sems['sHM'].sem, 4 * (fp - 1))
                V.op('tensor_copy', w1h[:, (fp % 2) * 1024:(fp % 2) * 1024 + 1024],
                     w1s[:, (fp % 2) * 1024:(fp % 2) * 1024 + 1024])
                tok_w1s = V.op('tensor_tensor',
                               w1l[:, (fp % 2) * 1024:(fp % 2) * 1024 + 1024],
                               w1s[:, (fp % 2) * 1024:(fp % 2) * 1024 + 1024],
                               w1h[:, (fp % 2) * 1024:(fp % 2) * 1024 + 1024],
                               op=ALU.subtract)
                wait_tok(V, w2_toks[fp])
                if fg >= 1:
                    V.wait_ge(sems['sWM2'].sem, 16 * fg)
                V.op('tensor_copy', w2h[:, fp2 * 1024:fp2 * 1024 + 1024],
                     w2s[:, fp2 * 1024:fp2 * 1024 + 1024])
                tok_w2s = V.op('tensor_tensor',
                               w2l[:, fp2 * 1024:fp2 * 1024 + 1024],
                               w2s[:, fp2 * 1024:fp2 * 1024 + 1024],
                               w2h[:, fp2 * 1024:fp2 * 1024 + 1024],
                               op=ALU.subtract)
                w2_toks[fp] = (V.c, tok_w2s)
                for tc4 in range(4):
                    m = 4 * fp + tc4
                    t_.wait_ge(V.c.sem, tok_w1s)
                    if fp == 0 and tc4 == 0:
                        wait_tok(t_, (V.c, tok_lnT_all))
                        t_.wait_ge(SC.c.sem, hsS_afterA)
                    if m >= 2:
                        wait_tok(t_, (SC.c, tok_ge[m - 2]))
                    first = True
                    for dp in range(8):
                        w1hp = w1h[:, (fp % 2) * 1024 + dp * 128:
                                   (fp % 2) * 1024 + dp * 128 + 128]
                        w1lp = w1l[:, (fp % 2) * 1024 + dp * 128:
                                   (fp % 2) * 1024 + dp * 128 + 128]
                        lh = lnh[:, dp * 2048 + tc4 * 512: dp * 2048 + tc4 * 512 + 512]
                        ll = lnl[:, dp * 2048 + tc4 * 512: dp * 2048 + tc4 * 512 + 512]
                        for a, bb, last in ((w1hp, lh, False), (w1hp, ll, False),
                                            (w1lp, lh, dp == 7)):
                            mm = t_.matmul(ph[m % 2][:, :], a, bb,
                                           start=first, stop=last)
                            first = False
                    sems['sHM'].inc(mm)
                    SC.wait_ge(sems['sHM'].sem, sems['sHM'].n)
                    if fg >= 1 and tc4 == 0 and fp2 == 0:
                        SC.wait_ge(sems['sWM2'].sem, 16 * fg)
                    if m >= 2:
                        SC.wait_ge(V.c.sem, tok_gsp.get(m - 2, 0))
                    tok_ge[m] = SC.op(
                        'activation', gtmp[:, (m % 2) * 512:(m % 2) * 512 + 512],
                        ph[m % 2][:, :], AF.Gelu_apprx_tanh, bias=b1c[:, fp:fp + 1])
                    V.wait_ge(SC.c.sem, tok_ge[m])
                    V.op('tensor_copy',
                         hTh[:, fp2 * 2048 + tc4 * 512: fp2 * 2048 + tc4 * 512 + 512],
                         gtmp[:, (m % 2) * 512:(m % 2) * 512 + 512])
                    tok_gsp[m] = V.op(
                        'tensor_tensor',
                        hTl[:, fp2 * 2048 + tc4 * 512: fp2 * 2048 + tc4 * 512 + 512],
                        gtmp[:, (m % 2) * 512:(m % 2) * 512 + 512],
                        hTh[:, fp2 * 2048 + tc4 * 512: fp2 * 2048 + tc4 * 512 + 512],
                        op=ALU.subtract)
            for t16 in range(NT):
                w = 16 * fg + t16
                wait_tok(t_, (V.c, tok_gsp[4 * (2 * fg + 1) + 3]))
                wait_tok(t_, w2_toks[2 * fg])
                wait_tok(t_, w2_toks[2 * fg + 1])
                if w >= 2:
                    wait_tok(t_, (V.c, tok_ya[w - 2]))
                for nh in range(2):
                    first = True
                    for fp2 in range(2):
                        hhp = hTh[:, fp2 * 2048 + t16 * 128: fp2 * 2048 + t16 * 128 + 128]
                        hlp = hTl[:, fp2 * 2048 + t16 * 128: fp2 * 2048 + t16 * 128 + 128]
                        w2hp = w2h[:, fp2 * 1024 + nh * 512: fp2 * 1024 + nh * 512 + 512]
                        w2lp = w2l[:, fp2 * 1024 + nh * 512: fp2 * 1024 + nh * 512 + 512]
                        for a, bb, last in ((hhp, w2hp, False), (hhp, w2lp, False),
                                            (hlp, w2hp, fp2 == 1)):
                            mm = t_.matmul(py[w % 2][:, nh * 512:nh * 512 + 512],
                                           a, bb, start=first, stop=last)
                            first = False
                sems['sWM2'].inc(mm)
                V.wait_ge(sems['sWM2'].sem, sems['sWM2'].n)
                if fg == 0:
                    tok_ya[w] = V.op('tensor_copy', yacc[:, t16 * 1024:(t16 + 1) * 1024],
                                     py[w % 2][:, :])
                else:
                    tok_ya[w] = V.op('tensor_tensor', yacc[:, t16 * 1024:(t16 + 1) * 1024],
                                     yacc[:, t16 * 1024:(t16 + 1) * 1024],
                                     py[w % 2][:, :], op=ALU.add)

        tok_oadd = {}
        tok_osc = {}
        for t in range(NT):
            if t >= 2:
                wait_tok(y_, (V.c, tok_oadd[t - 2]))
            c_tok = streams['a'].issue(
                y_, [(x1t[:, (t % 2) * 1024:(t % 2) * 1024 + 1024], x1_r[t])])
            if t >= 1:
                V.wait_ge(SC.c.sem, tok_osc[t - 1])
            t_tmp = V.op('tensor_tensor', tmpb[:], yacc[:, t * 1024:(t + 1) * 1024],
                         b2b[:], op=ALU.add)
            SC.wait_ge(V.c.sem, t_tmp)
            SC.wait_ge(V.c.sem, tok_fc[NT - 1])
            if t >= 2:
                wait_tok(SC, ost_toks[t - 2])
            tok_osc[t] = SC.op(
                'activation', outb[:, (t % 2) * 1024:(t % 2) * 1024 + 1024], tmpb[:],
                AF.Identity, scale=fbuf[:, t:t + 1], bias=cst[:, C_ZERO:C_ZERO + 1])
            V.wait_ge(SC.c.sem, tok_osc[t])
            V.wait_tok(c_tok)
            tok_oadd[t] = V.op(
                'tensor_tensor', outb[:, (t % 2) * 1024:(t % 2) * 1024 + 1024],
                outb[:, (t % 2) * 1024:(t % 2) * 1024 + 1024],
                x1t[:, (t % 2) * 1024:(t % 2) * 1024 + 1024], op=ALU.add)
            g_.wait_ge(V.c.sem, tok_oadd[t])
            ost_toks[t] = streams['ost'].issue(
                g_, [(out_r[t], outb[:, (t % 2) * 1024:(t % 2) * 1024 + 1024])])


_NC_CACHE = {}


def kernel(L0_attn_x, L0_attn_gamma, L0_attn_w_qkv, L0_attn_w_out,
           L0_ffn_router_w, L0_ffn_w1, L0_ffn_b1, L0_ffn_w2, L0_ffn_b2):
    from concourse.bass_utils import run_bass_kernel_spmd
    x_full = np.asarray(L0_attn_x, np.float32)
    n = 8
    base = host_inputs(x_full[0], L0_attn_gamma, L0_attn_w_qkv, L0_attn_w_out,
                       L0_ffn_router_w, L0_ffn_w1, L0_ffn_b1, L0_ffn_w2, L0_ffn_b2)
    in_maps = []
    for b in range(n):
        m = dict(base)
        m['x'] = np.ascontiguousarray(x_full[b])
        in_maps.append(m)
    if 'nc' not in _NC_CACHE:
        _NC_CACHE['nc'] = build()
    res = run_bass_kernel_spmd(_NC_CACHE['nc'], in_maps, core_ids=list(range(n)))
    out = np.stack([np.asarray(r['out'], np.float32) for r in res.results], axis=0)
    return out
